# version 15 (final)
# speedup vs baseline: 1.1632x; 1.0502x over previous
import numpy as np
import ml_dtypes
import concourse.bass as bass
import concourse.mybir as mybir
from concourse.bass_utils import run_bass_kernel_spmd

F32 = mybir.dt.float32
BF16 = mybir.dt.bfloat16
FP16 = mybir.dt.float16
I32 = mybir.dt.int32
AF = mybir.ActivationFunctionType
ALU = mybir.AluOpType
AXL = mybir.AxisListType

STAGE = 3

D = 1024
INNER = 4096
H = 16
DH = 64
LAT = 256
T = 2048
S = 4096
NT = T // 128
N_CORES = 8
TB = 512
NTB = T // TB
EPS = 1e-5
KB = 1024


class Sem:
    def __init__(self, handle, name):
        self.h, self.name, self.n = handle, name, 0


class Prog:

    ENGINES = ("t", "v", "s", "g", "y")

    def __init__(self, nc):
        self.nc = nc
        self.q = {e: [] for e in self.ENGINES}
        self._cms = []
        self.auto = {}

    def sem(self, name):
        cm = self.nc.semaphore(name)
        h = cm.__enter__()
        self._cms.append(cm)
        return Sem(h, name)

    def op(self, eng, build, waits=(), inc=None):
        own = self.auto.get(eng)
        waits = [(s, v) for (s, v) in waits if v > 0]
        if own is not None:
            if own.n > 0 and not any(s is own and v >= own.n for s, v in waits):
                waits.append((own, own.n))
            if inc is None:
                inc = (own, 1)
        if inc is not None:
            inc[0].n += inc[1]
        self.q[eng].append((build, waits, inc))
        return None if inc is None else inc[0].n

    def emit(self):
        with self.nc.Block() as block:
            sections = {"t": block.tensor, "v": block.vector, "s": block.scalar,
                        "g": block.gpsimd, "y": block.sync}
            for key in self.ENGINES:
                ops = self.q[key]
                if not ops:
                    continue

                def body(e, ops=ops):
                    for build, waits, inc in ops:
                        for sem, v in waits:
                            e.wait_ge(sem.h, v)
                        instr = build(e)
                        if inc is not None:
                            instr.then_inc(inc[0].h, inc[1])

                sections[key](body)


def build_nc():
    nc = bass.Bass("TRN2", target_bir_lowering=False)
    p = Prog(nc)

    dx = nc.dram_tensor("x", [T, D], F32, kind="ExternalInput").ap()
    dy = nc.dram_tensor("y", [T, D], F32, kind="ExternalOutput").ap()
    dwg = nc.dram_tensor("w_gate", [INNER, D], F32, kind="ExternalInput").ap()
    dwu = nc.dram_tensor("w_up", [INNER, D], F32, kind="ExternalInput").ap()
    dwd = nc.dram_tensor("w_down", [D, INNER], F32, kind="ExternalInput").ap()
    d_idb = nc.dram_tensor("idn_bf16", [128, 128], BF16, kind="ExternalInput").ap()
    dx_cur = nc.dram_tensor("x_cur", [T, D], F32).ap()
    dwgT = nc.dram_tensor("wgT", [8 * 8 * 128, 512], BF16).ap()
    dwuT = nc.dram_tensor("wuT", [8 * 8 * 128, 512], BF16).ap()
    dwdT = nc.dram_tensor("wdT", [2 * 32 * 128, 512], BF16).ap()
    dq2T = [nc.dram_tensor(f"q2T{i}", [NTB * 32 * 128, TB], BF16).ap()
            for i in range(2)]
    if STAGE >= 2:
        dwq = nc.dram_tensor("Wq", [D, D], F32, kind="ExternalInput").ap()
        dwl = nc.dram_tensor("Wl", [D, LAT], F32, kind="ExternalInput").ap()
        dwk = nc.dram_tensor("Wk", [LAT, D], F32, kind="ExternalInput").ap()
        dwv = nc.dram_tensor("Wv", [LAT, D], F32, kind="ExternalInput").ap()
        dwo = nc.dram_tensor("Wo", [D, D], F32, kind="ExternalInput").ap()
        dbq = nc.dram_tensor("bq", [D], F32, kind="ExternalInput").ap()
        dbl = nc.dram_tensor("bl", [LAT], F32, kind="ExternalInput").ap()
        dbk = nc.dram_tensor("bk", [D], F32, kind="ExternalInput").ap()
        dbv = nc.dram_tensor("bv", [D], F32, kind="ExternalInput").ap()
        dbo = nc.dram_tensor("bo", [D], F32, kind="ExternalInput").ap()
        d_idf = nc.dram_tensor("idn_f32", [128, 128], F32, kind="ExternalInput").ap()
        lat_cc_in = nc.dram_tensor("lat_cc_in", [4 * 128, T], BF16)
        lat_cc_out = nc.dram_tensor("lat_cc_out", [8 * 128, T], BF16)
        doT_hi = nc.dram_tensor("oT_hi", [D, T], BF16).ap()
        doT_lo = nc.dram_tensor("oT_lo", [D, T], BF16).ap()
        dx_cur2 = nc.dram_tensor("x_cur2", [T, D], F32).ap()

    s_misc = p.sem("s_misc")
    s_x = [p.sem(f"s_x{i}") for i in range(2)]
    s_w = [p.sem(f"s_w{i}") for i in range(2)]
    s_wg = [p.sem(f"s_wg{i}") for i in range(2)]
    s_wu = [p.sem(f"s_wu{i}") for i in range(2)]
    s_wd = p.sem("s_wd")
    s_lx = [p.sem(f"s_lx{i}") for i in range(2)]
    s_sto = [p.sem(f"s_sto{i}") for i in range(2)]
    s_pst = [p.sem(f"s_pst{i}") for i in range(2)]
    s_q2s = p.sem("s_q2s")
    s_q2l = p.sem("s_q2l")
    s_pe = p.sem("s_pe")
    s_ve = p.sem("s_ve")
    s_se = p.sem("s_se")
    s_gc = p.sem("s_gc")
    s_x2 = p.sem("s_x2")
    s_wp = p.sem("s_wp")
    s_cin = p.sem("s_cin")
    s_cc = p.sem("s_cc")
    s_lat = p.sem("s_lat")
    s_ot = [p.sem(f"s_ot{i}") for i in range(2)]
    s_o3 = [p.sem(f"s_o3{i}") for i in range(2)]
    s_rb = p.sem("s_rb")
    s_wo = p.sem("s_wo")
    p.auto = {"v": s_ve, "s": s_se, "g": s_gc}

    def sb(name, shape, dtype, off):
        assert off + int(np.prod(shape[1:])) * mybir.dt.size(dtype) <= 224 * KB, name
        return nc.alloc_sbuf_tensor_at(name, shape, dtype, offset=off)

    _cur = [24 * KB]

    def sbp(name, shape, dtype):
        off = (_cur[0] + 31) // 32 * 32
        _cur[0] = off + int(np.prod(shape[1:])) * mybir.dt.size(dtype)
        assert _cur[0] <= 28 * KB
        return sb(name, shape, dtype, off)

    c_misc = sbp("c_misc", [128, 16], F32)
    invact1 = sbp("invact1", [128, NT], F32)
    scg_all = sbp("scg_all", [128, NT], F32)
    scu_all = sbp("scu_all", [128, NT], F32)
    scd_all = sbp("scd_all", [128, NT], F32)
    ws_g = sbp("ws_g", [128, 1], F32)
    ws_u = sbp("ws_u", [128, 1], F32)
    ws_d = sbp("ws_d", [128, 1], F32)
    iws_g = sbp("iws_g", [128, 1], F32)
    iws_u = sbp("iws_u", [128, 1], F32)
    iws_d = sbp("iws_d", [128, 1], F32)
    ones_row = sbp("ones_row", [1, 128], F32)
    ones_col = sbp("ones_col", [128, 1], F32)
    accp = sbp("accp", [128, 72], F32)
    idn_b = sbp("idn_b", [128, 128], BF16)
    ln_mean = sbp("ln_mean", [128, NT], F32)
    ln_rstd = sbp("ln_rstd", [128, NT], F32)
    q_sc = sbp("q_sc", [128, 8], F32)
    st6 = sbp("st6", [128, 16], F32)
    mv2 = sbp("mv2", [128, 2], F32)
    qsc2 = sbp("qsc2", [128, 8], F32)
    c_eps = sbp("c_eps", [128, 1], F32)
    xq1T = sb("xq1T", [128, 8, T], BF16, 28 * KB)

    A = 60 * KB
    gu = [sb(f"gu{t}", [128, INNER], F32, A + t * 16 * KB) for t in range(4)]
    q2T = sb("q2T", [128, 32, TB], BF16, A + 64 * KB)
    q2tm = sb("q2tm", [128, INNER], BF16, A + 96 * KB)
    wsl = [sb(f"wsl{i}", [128, 8, 512], BF16, A + 104 * KB + i * 8 * KB)
           for i in range(4)]
    wdl = sb("wdl", [128, 32, 512], BF16, A + 104 * KB)
    sg = [sb(f"sg{i}", [128, 512], F32, A + 136 * KB + i * 2 * KB) for i in range(2)]
    su = [sb(f"su{i}", [128, 512], F32, A + 140 * KB + i * 2 * KB) for i in range(2)]
    dn = [sb(f"dn{i}", [128, 512], F32, A + 144 * KB + i * 2 * KB) for i in range(2)]
    xio = [sb(f"xio{i}", [128, 512], F32, A + 148 * KB + i * 2 * KB) for i in range(2)]
    qs_i = sb("qs_i", [128, 512], I32, A + 152 * KB)
    qs_b = [sb(f"qs_b{i}", [128, 512], F32, A + 154 * KB + i * 2 * KB)
            for i in range(2)]
    qs_e = [sb(f"qs_e{i}", [128, 512], F32, A + 158 * KB + i * 2 * KB)
            for i in range(2)]
    qs_c = [sb(f"qs_c{i}", [128, 512], F32, A + 144 * KB + i * 2 * KB)
            for i in range(2)]

    wbuf = [sb(f"wbuf{i}", [128, INNER], F32, A + i * 16 * KB) for i in range(2)]
    wscr = sb("wscr", [128, INNER], F32, A + 32 * KB)
    wgl = sb("wgl", [128, INNER], F32, A + 48 * KB)
    wqb = sb("wqb", [128, INNER], BF16, A + 64 * KB)
    asm = sb("asm", [128, 32, 512], BF16, A + 72 * KB)
    xc = [sb(f"xc{i}", [128, D], F32, A + 104 * KB + i * 4 * KB) for i in range(2)]
    g_qt_scr = (
        [sb("qt_f0", [128, D], F32, A + 112 * KB),
         sb("qt_f1", [128, D], F32, A + 136 * KB)],
        sb("qt_i", [128, D], I32, A + 116 * KB),
        sb("qt_e", [128, D], F32, A + 120 * KB),
        sb("qt_c", [128, D], F32, A + 124 * KB),
        sb("xq_tm", [128, D], BF16, A + 128 * KB),
        [sb("qt_b0", [128, D], F32, A + 132 * KB),
         sb("qt_b1", [128, D], F32, A + 140 * KB)])

    pb = [nc.alloc_psum_tensor(f"pb{i}", [128, 512], F32) for i in range(8)]
    pbb = [t.bitcast(BF16) for t in pb]
    bank_free = {i: [] for i in range(8)}

    c127 = c_misc[:, 0:1]
    c1 = c_misc[:, 1:2]

    def memset(ap, val):
        p.op("v", lambda e, ap=ap, val=val: e.memset(ap, val))

    memset(c_misc[:, 0:1], 127.0)
    memset(c_misc[:, 1:2], 1.0)
    memset(ones_row[:, :], 1.0)
    memset(ones_col[:, :], 1.0)
    memset(accp[:, :], 0.0)
    memset(c_eps[:, :], EPS)
    id_ld = p.op("y", lambda e: e.dma_start(out=idn_b[:, :], in_=d_idb),
                 inc=(s_misc, 16))

    x_src = dx.rearrange("(c p) d -> c p d", p=128)

    def ln_quant_p1(xbuf, invact_dst, ln_idx, first_waits=(), ow_waits=(),
                    scratch=None, qtb_free=None):
        qt_f, qt_i, _, _, _, qt_b = (scratch if scratch is not None
                                     else g_qt_scr)
        par = ln_idx % 2
        p.op("v", lambda e: e.bn_stats(st6[:, 0:6], xbuf[:, 0:512]),
             waits=list(first_waits))
        p.op("v", lambda e: e.bn_stats(st6[:, 6:12], xbuf[:, 512:1024]))
        v0 = p.op("v", lambda e: e.bn_aggr(mv2[:, :], st6[:, 0:12]), inc=(s_ve, 1))
        p.op("s", lambda e: e.activation(q_sc[:, 0:1], mv2[:, 1:2], AF.Ln,
                                         bias=c_eps[:, :]), waits=[(s_ve, v0)])
        s0 = p.op("s", lambda e: e.activation(ln_rstd[:, ln_idx:ln_idx + 1],
                                              q_sc[:, 0:1], AF.Exp, scale=-0.5),
                  inc=(s_se, 1))
        p.op("v", lambda e: e.tensor_copy(ln_mean[:, ln_idx:ln_idx + 1],
                                          mv2[:, 0:1]))
        p.op("v", lambda e: e.tensor_scalar(
            xbuf[:, :], xbuf[:, :], ln_mean[:, ln_idx:ln_idx + 1],
            ln_rstd[:, ln_idx:ln_idx + 1], ALU.subtract, ALU.mult),
            waits=[(s_se, s0)] + list(ow_waits))
        p.op("v", lambda e: e.reduce_max(q_sc[:, 1:2], xbuf[:, :], axis=AXL.X,
                                         apply_absolute_value=True))
        p.op("v", lambda e: e.tensor_single_scalar(q_sc[:, 2:3], q_sc[:, 1:2],
                                                   EPS, ALU.max))
        p.op("v", lambda e: e.reciprocal(q_sc[:, 6:7], q_sc[:, 2:3]))
        p.op("v", lambda e: e.tensor_single_scalar(q_sc[:, 3:4], q_sc[:, 6:7],
                                                   127.0, ALU.mult))
        p.op("v", lambda e: e.reciprocal(invact_dst, q_sc[:, 3:4]))
        vm = p.op("v", lambda e, par=par: e.tensor_single_scalar(
            qt_f[par][:, :], xbuf[:, :], q_sc[:, 3:4], ALU.mult),
            inc=(s_ve, 1))
        p.op("s", lambda e, par=par: e.activation(
            qt_i[:, :], qt_f[par][:, :], AF.Copy, bias=0.0, scale=1.0),
            waits=[(s_ve, vm)])
        sc2 = p.op("s", lambda e, par=par: e.activation(
            qt_b[par][:, :], qt_i[:, :], AF.Copy, bias=0.0, scale=1.0),
            waits=[(s_ve, qtb_free[par])], inc=(s_se, 1))
        return vm, sc2

    def ln_quant_p2(xqT_dst, tc, ln_idx, sc2, scratch=None, qtb_free=None):
        qt_f, _, qt_e, qt_c, xq_tm, qt_b = (scratch if scratch is not None
                                            else g_qt_scr)
        par = ln_idx % 2
        p.op("v", lambda e, par=par: e.tensor_tensor(
            qt_e[:, :], qt_f[par][:, :], qt_b[par][:, :], ALU.subtract),
            waits=[(s_se, sc2)])
        p.op("v", lambda e, par=par: e.scalar_tensor_tensor(
            qt_c[:, :], qt_e[:, :], 0.5, qt_b[par][:, :], ALU.is_gt, ALU.add))
        stt = p.op("v", lambda e: e.tensor_single_scalar(
            qt_e[:, :], qt_e[:, :], -0.5, ALU.is_lt), inc=(s_ve, 1))
        qtb_free[par] = stt
        vq = p.op("v", lambda e: e.tensor_tensor(xq_tm[:, :], qt_c[:, :],
                                                 qt_e[:, :], ALU.subtract),
                  inc=(s_ve, 1))
        last_ev = 0
        for half in range(2):
            m = None
            for j in range(4):
                ic = half * 4 + j
                w = ([(s_ve, vq), (s_misc, id_ld)] + bank_free[half]) if j == 0 else []
                m = p.op("t", lambda e, half=half, j=j, ic=ic: e.transpose(
                    pbb[half][:, j * 128:(j + 1) * 128],
                    xq_tm[:, ic * 128:(ic + 1) * 128], idn_b[:, :]),
                    waits=w, inc=((s_pe, 1) if j == 3 else None))
            ev = p.op("v", lambda e, half=half, tc=tc: e.tensor_copy(
                xqT_dst[:, half * 4:(half + 1) * 4, tc * 128:(tc + 1) * 128],
                pbb[half][:, 0:512]), waits=[(s_pe, m)], inc=(s_ve, 1))
            bank_free[half] = [(s_ve, ev)]
            last_ev = ev
        return vq, last_ev

    xbuf_free = {0: [], 1: []}
    qtb_free1 = {0: 0, 1: 0}
    xq_ev1 = []
    pend_ln = None
    for tc in range(NT):
        ld = p.op("y", lambda e, tc=tc: e.dma_start(
            out=xc[tc % 2][:, :], in_=x_src[tc]), waits=xbuf_free[tc % 2],
            inc=(s_x[tc % 2], 16))
        vm, sc2 = ln_quant_p1(xc[tc % 2], invact1[:, tc:tc + 1], tc,
                              first_waits=[(s_x[tc % 2], ld)], qtb_free=qtb_free1)
        if pend_ln is not None:
            _, ev = ln_quant_p2(xq1T, pend_ln[0], pend_ln[0], pend_ln[1],
                                qtb_free=qtb_free1)
            xq_ev1.append(ev)
        pend_ln = (tc, sc2)
        xbuf_free[tc % 2] = [(s_ve, vm)]
    _, ev = ln_quant_p2(xq1T, pend_ln[0], pend_ln[0], pend_ln[1],
                        qtb_free=qtb_free1)
    xq_ev1.append(ev)
    xq_done = ev

    mats = [("g", dwg, 32, D, dwgT, ws_g, iws_g, 0),
            ("u", dwu, 32, D, dwuT, ws_u, iws_u, 32),
            ("d", dwd, 8, INNER, dwdT, ws_d, iws_d, 64)]

    abs_done = {}
    smile = []
    for name, dw, n_chunks, C, _, _, _, col0 in mats:
        src = dw.rearrange("(c p) d -> c p d", p=128)
        for r in range(n_chunks):
            gi = len(smile)
            buf = wbuf[gi % 2]
            w = [(s_se, smile[gi - 2])] if gi >= 2 else []
            ld = p.op("y", lambda e, src=src, r=r, buf=buf, C=C: e.dma_start(
                out=buf[:, 0:C], in_=src[r]), waits=w, inc=(s_w[gi % 2], 16))
            sm = p.op("s", lambda e, buf=buf, C=C, col=col0 + r: e.activation(
                wscr[:, 0:C], buf[:, 0:C], AF.Abs,
                accum_out=accp[:, col:col + 1]),
                waits=[(s_w[gi % 2], ld)], inc=(s_se, 1))
            smile.append(sm)
        abs_done[name] = smile[-1]

    for name, dw, n_chunks, C, dwT, ws_dst, iws_dst, col0 in mats:
        v0 = p.op("v", lambda e, col0=col0, n=n_chunks: e.reduce_sum(
            q_sc[:, 4:5], accp[:, col0:col0 + n], axis=AXL.X),
            waits=[(s_se, abs_done[name])], inc=(s_ve, 1))
        m0 = p.op("t", lambda e: e.matmul(pb[6][0:1, 0:1], ones_col[:, :],
                                          q_sc[:, 4:5], start=True, stop=True),
                  waits=[(s_ve, v0)] + bank_free[6], inc=(s_pe, 1))
        nel = float(n_chunks * 128 * C)
        p.op("v", lambda e, nel=nel: e.tensor_single_scalar(
            q_sc[0:1, 5:6], pb[6][0:1, 0:1], 1.0 / nel, ALU.mult),
            waits=[(s_pe, m0)])
        v1 = p.op("v", lambda e: e.tensor_single_scalar(
            q_sc[0:1, 5:6], q_sc[0:1, 5:6], EPS, ALU.max), inc=(s_ve, 1))
        m1 = p.op("t", lambda e: e.matmul(pb[6][:, 1:2], ones_row[:, :],
                                          q_sc[0:1, 5:6], start=True, stop=True),
                  waits=[(s_ve, v1)], inc=(s_pe, 1))
        p.op("v", lambda e, iws=iws_dst: e.tensor_copy(iws[:, :], pb[6][:, 1:2]),
             waits=[(s_pe, m1)])
        vw = p.op("v", lambda e, ws=ws_dst, iws=iws_dst: e.reciprocal(ws[:, :], iws[:, :]),
                  inc=(s_ve, 1))
        bank_free[6] = [(s_ve, vw)]
    ws_ready = s_ve.n

    prep_vq = []
    prep_st = {}
    gg = 0
    for name, dw, n_chunks, C, dwT, ws_src, _, _ in mats:
        src = dw.rearrange("(c p) d -> c p d", p=128)
        n_ic = C // 128
        wide = (C == INNER)
        prep_st[name] = []
        for g in range(n_chunks // 4):
            for j in range(4):
                r = g * 4 + j
                gi = len(smile)
                buf = wbuf[gi % 2]
                par = 0 if wide else (r % 2)
                a0 = par * C
                w = [(s_se, smile[gi - 2])]
                if gi == 72:
                    w.append((s_ve, ws_ready))
                ld = p.op("y", lambda e, src=src, r=r, buf=buf, C=C: e.dma_start(
                    out=buf[:, 0:C], in_=src[r]), waits=w, inc=(s_w[gi % 2], 16))
                vq_w = prep_vq[-2:] if (wide or len(prep_st[name]) == 0 and
                                        g == 0 and j < 2) else prep_vq[-2:-1]
                sm = p.op("s", lambda e, buf=buf, C=C, ws=ws_src, a0=a0:
                          e.activation(
                              wscr[:, a0:a0 + C], buf[:, 0:C], AF.Copy,
                              scale=ws[:, :]),
                          waits=[(s_w[gi % 2], ld)] + vq_w, inc=(s_se, 1))
                smile.append(sm)
                p.op("v", lambda e, C=C, a0=a0: e.tensor_single_scalar(
                    wgl[:, a0:a0 + C], wscr[:, a0:a0 + C], 0.5, ALU.is_gt),
                    waits=[(s_se, sm)])
                p.op("v", lambda e, C=C, a0=a0: e.tensor_single_scalar(
                    wscr[:, a0:a0 + C], wscr[:, a0:a0 + C], -0.5, ALU.is_lt))
                vq = p.op("v", lambda e, C=C, a0=a0: e.tensor_tensor(
                    wqb[:, a0:a0 + C], wgl[:, a0:a0 + C], wscr[:, a0:a0 + C],
                    ALU.subtract),
                    inc=(s_ve, 1))
                prep_vq.append((s_ve, vq))
                for ic in range(n_ic):
                    bank = 2 + (ic % 2)
                    w2 = ([(s_ve, vq)] if ic < 2 else []) + bank_free[bank]
                    m = p.op("t", lambda e, bank=bank, ic=ic, a0=a0: e.transpose(
                        pbb[bank][:, 0:128],
                        wqb[:, a0 + ic * 128:a0 + (ic + 1) * 128],
                        idn_b[:, :]), waits=w2, inc=(s_pe, 1))
                    asm_w = [(s_pe, m)]
                    if j == 0 and ic == 0:
                        sts = prep_st[name]
                        if wide or len(sts) < 2:
                            asm_w += [(sp, sp.n) for sp in s_pst if sp.n > 0]
                        else:
                            asm_w.append(sts[-2])
                    slot = (0 if wide else (g % 2) * n_ic) + ic
                    ev = p.op("v", lambda e, slot=slot, j=j, bank=bank:
                              e.tensor_copy(
                                  asm[:, slot, j * 128:(j + 1) * 128],
                                  pbb[bank][:, 0:128]),
                              waits=asm_w, inc=(s_ve, 1))
                    bank_free[bank] = [(s_ve, ev)]
            asm_done = s_ve.n
            for ic in range(n_ic):
                slot = (0 if wide else (g % 2) * n_ic) + ic
                p.op("g", lambda e, dwT=dwT, g=g, ic=ic, n_ic=n_ic, slot=slot:
                     e.dma_start(
                         out=dwT[(g * n_ic + ic) * 128:
                                 (g * n_ic + ic + 1) * 128, :],
                         in_=asm[:, slot, :]),
                     waits=[(s_ve, asm_done)] if ic == 0 else [],
                     inc=(s_pst[gg % 2], 16))
            prep_st[name].append((s_pst[gg % 2], s_pst[gg % 2].n))
            gg += 1
    prep_done_gd = [(sp, sp.n) for sp in s_pst]
    prep_s_last = smile[-1]
    prep_v_last = prep_vq[-1]

    def emit_ffn(xqT, x_in_src, x_out_dst, extra_out_dst, scg, scu, scd,
                 first_waits, xq_ev, dq2, sc_gate=(), gu_gate=(), ev_gate=(),
                 wst=None):
        xin = x_in_src.rearrange("(c p) d -> c p d", p=128)
        xout = x_out_dst.rearrange("(c p) d -> c p d", p=128)
        xtra = (extra_out_dst.rearrange("(c p) d -> c p d", p=128)
                if extra_out_dst is not None else None)
        wg_src = dwgT.rearrange("(g a p) n -> g p a n", g=8, p=128)
        wu_src = dwuT.rearrange("(g a p) n -> g p a n", g=8, p=128)
        wd_src = dwdT.rearrange("(g a p) n -> g p a n", g=2, p=128)
        q2_st = dq2.rearrange("(b a p) n -> b p a n", b=NTB, p=128)

        mm_after = {}
        sgsu_free = {}
        q2s_done = {}
        qsb_free = {0: 0, 1: 0}
        qec_free = {0: 0, 1: 0}
        first = True
        first_s1 = list(sc_gate)
        first_vg = list(gu_gate)
        first_ev = list(ev_gate)
        for tb in range(NTB):
            for ot in range(8):
                gi = tb * 8 + ot
                gb, ub = wsl[ot % 2], wsl[2 + (ot % 2)]
                if first:
                    w = list(first_waits) + [(s_misc, id_ld)]
                    first = False
                elif gi >= 2:
                    w = [(s_pe, mm_after[gi - 2])]
                else:
                    w = []
                if wst is not None and tb == 0:
                    w = w + [wst["g"][ot]]
                lg = p.op("y", lambda e, gb=gb, ot=ot: e.dma_start(
                    out=gb[:, :, :], in_=wg_src[ot]), waits=w,
                    inc=(s_wg[ot % 2], 16))
                lu = p.op("y", lambda e, ub=ub, ot=ot: e.dma_start(
                    out=ub[:, :, :], in_=wu_src[ot]),
                    waits=([wst["u"][ot]]
                           if wst is not None and tb == 0 else []),
                    inc=(s_wu[ot % 2], 16))
                for tt in range(4):
                    tc = tb * 4 + tt
                    for bank, wslice, lsem in ((tt % 2, gb, (s_wg[ot % 2], lg)),
                                               (2 + (tt % 2), ub, (s_wu[ot % 2], lu))):
                        for ic in range(8):
                            wts = ([lsem, (s_ve, xq_ev[tc])] +
                                   bank_free[bank]) if ic == 0 else []
                            p.op("t", lambda e, bank=bank, wslice=wslice, ic=ic,
                                 tc=tc: e.matmul(
                                     pb[bank][:, :],
                                     xqT[:, ic, tc * 128:(tc + 1) * 128],
                                     wslice[:, ic, :],
                                     start=(ic == 0), stop=(ic == 7)),
                                 waits=wts, inc=((s_pe, 1) if ic == 7 else None))
                    mm_done = s_pe.n
                    par = tt % 2
                    s1 = p.op("s", lambda e, b=par, par=par, tc=tc:
                              e.activation(sg[par][:, :], pb[b][:, :], AF.Silu,
                                           scale=scg[:, tc:tc + 1]),
                              waits=[(s_pe, mm_done)] + sgsu_free.get(par, []) +
                              first_s1, inc=(s_se, 1))
                    first_s1 = []
                    v1 = p.op("s", lambda e, b=2 + par, par=par, tc=tc:
                              e.activation(su[par][:, :], pb[b][:, :], AF.Copy,
                                           scale=scu[:, tc:tc + 1]),
                              waits=[(s_pe, mm_done)], inc=(s_se, 1))
                    bank_free[par] = [(s_se, s1)]
                    bank_free[2 + par] = [(s_se, v1)]
                    vg = p.op("g", lambda e, par=par, tt=tt, ot=ot: e.tensor_tensor(
                        gu[tt][:, ot * 512:(ot + 1) * 512], sg[par][:, :],
                        su[par][:, :], ALU.mult),
                        waits=[(s_se, v1)] + first_vg, inc=(s_gc, 1))
                    first_vg = []
                    sgsu_free[par] = [(s_gc, vg)]
                mm_after[gi] = s_pe.n
            for tt in range(4):
                tc = tb * 4 + tt
                p.op("v", lambda e, tt=tt: e.reduce_max(
                    qsc2[:, 0:1], gu[tt][:, :], axis=AXL.X,
                    apply_absolute_value=True),
                    waits=[(s_gc, s_gc.n)])
                p.op("v", lambda e: e.tensor_single_scalar(
                    qsc2[:, 1:2], qsc2[:, 0:1], EPS, ALU.max))
                p.op("v", lambda e: e.reciprocal(qsc2[:, 6:7], qsc2[:, 1:2]))
                p.op("v", lambda e: e.tensor_single_scalar(
                    qsc2[:, 2:3], qsc2[:, 6:7], 127.0, ALU.mult))
                p.op("v", lambda e: e.reciprocal(qsc2[:, 3:4], qsc2[:, 2:3]))
                p.op("v", lambda e: e.tensor_tensor(
                    qsc2[:, 4:5], qsc2[:, 3:4], iws_d[:, :], ALU.mult))
                p.op("v", lambda e, tc=tc: e.tensor_single_scalar(
                    scd[:, tc:tc + 1], qsc2[:, 4:5], 0.5, ALU.mult))
                vq_sl = []
                for sl in range(8):
                    a, b2 = sl * 512, (sl + 1) * 512
                    par2 = sl % 2
                    v1 = p.op("v", lambda e, tt=tt, a=a, b2=b2:
                              e.tensor_single_scalar(
                                  gu[tt][:, a:b2], gu[tt][:, a:b2],
                                  qsc2[:, 2:3], ALU.mult), inc=(s_ve, 1))
                    p.op("s", lambda e, tt=tt, a=a, b2=b2: e.activation(
                        qs_i[:, :], gu[tt][:, a:b2], AF.Copy,
                        bias=0.0, scale=1.0), waits=[(s_ve, v1)])
                    s3 = p.op("s", lambda e, par2=par2: e.activation(
                        qs_b[par2][:, :], qs_i[:, :], AF.Copy,
                        bias=0.0, scale=1.0),
                        waits=[(s_ve, qsb_free[par2])], inc=(s_se, 1))
                    p.op("v", lambda e, tt=tt, a=a, b2=b2, par2=par2:
                         e.tensor_tensor(
                             qs_e[par2][:, :], gu[tt][:, a:b2],
                             qs_b[par2][:, :], ALU.subtract),
                         waits=[(s_se, s3), (s_gc, qec_free[par2])])
                    p.op("v", lambda e, par2=par2: e.scalar_tensor_tensor(
                        qs_c[par2][:, :], qs_e[par2][:, :], 0.5,
                        qs_b[par2][:, :], ALU.is_gt, ALU.add))
                    v6 = p.op("v", lambda e, par2=par2: e.tensor_single_scalar(
                        qs_e[par2][:, :], qs_e[par2][:, :], -0.5,
                        ALU.is_lt), inc=(s_ve, 1))
                    qsb_free[par2] = v6
                    vq = p.op("g", lambda e, a=a, b2=b2, par2=par2:
                              e.tensor_tensor(
                                  q2tm[:, a:b2], qs_c[par2][:, :],
                                  qs_e[par2][:, :], ALU.subtract),
                              waits=[(s_ve, v6)], inc=(s_gc, 1))
                    qec_free[par2] = vq
                    vq_sl.append([(s_gc, vq)])
                for half in range(8):
                    bank = 4 + (half % 2)
                    m = None
                    for j in range(4):
                        ic = half * 4 + j
                        w3 = (vq_sl[half] + [(s_misc, id_ld), (s_q2s, s_q2s.n)] +
                              bank_free[bank]) if j == 0 else []
                        m = p.op("t", lambda e, bank=bank, j=j, ic=ic:
                                 e.transpose(pbb[bank][:, j * 128:(j + 1) * 128],
                                             q2tm[:, ic * 128:(ic + 1) * 128],
                                             idn_b[:, :]),
                                 waits=w3, inc=((s_pe, 1) if j == 3 else None))
                    ev = p.op("v", lambda e, bank=bank, half=half, tt=tt:
                              e.tensor_copy(
                                  q2T[:, half * 4:(half + 1) * 4,
                                      tt * 128:(tt + 1) * 128],
                                  pbb[bank][:, 0:512]),
                              waits=[(s_pe, m)] + first_ev, inc=(s_ve, 1))
                    first_ev = []
                    bank_free[bank] = [(s_ve, ev)]
            st = p.op("g", lambda e, tb=tb: e.dma_start(
                out=q2_st[tb], in_=q2T[:, :, :]),
                waits=[(s_ve, s_ve.n)], inc=(s_q2s, 16))
            q2s_done[tb] = st
        xio_free = {}
        q2l_done = {}
        for ot2 in range(2):
            ldd = p.op("y", lambda e, ot2=ot2: e.dma_start(
                out=wdl[:, :, :], in_=wd_src[ot2]),
                waits=([(s_pe, s_pe.n), (s_q2s, q2s_done[0])] if ot2 == 0
                       else [(s_pe, s_pe.n)]) +
                ([wst["d"][ot2]] if wst is not None else []),
                inc=(s_wd, 16))
            for tb in range(NTB):
                lq = p.op("y", lambda e, tb=tb: e.dma_start(
                    out=q2T[:, :, :], in_=q2_st[tb]),
                    waits=[(s_q2s, q2s_done[NTB - 1]), (s_pe, s_pe.n)],
                    inc=(s_q2l, 16))
                for tt in range(4):
                    tc = tb * 4 + tt
                    bank = 6 + (tt % 2)
                    for ic in range(32):
                        wts = ([(s_wd, ldd), (s_q2l, lq)] + bank_free[bank]) \
                            if ic == 0 else []
                        p.op("t", lambda e, bank=bank, ic=ic, tt=tt: e.matmul(
                            pb[bank][:, :], q2T[:, ic, tt * 128:(tt + 1) * 128],
                            wdl[:, ic, :], start=(ic == 0), stop=(ic == 31)),
                            waits=wts, inc=((s_pe, 1) if ic == 31 else None))
                    mmd = s_pe.n
                    par = tt % 2
                    lx = p.op("y", lambda e, tc=tc, ot2=ot2, par=par: e.dma_start(
                        out=xio[par][:, :], in_=xin[tc][:, ot2 * 512:(ot2 + 1) * 512]),
                        waits=xio_free.get(par, []), inc=(s_lx[par], 16))
                    s1 = p.op("s", lambda e, bank=bank, par=par, tc=tc:
                              e.activation(dn[par][:, :], pb[bank][:, :],
                                           AF.Copy, scale=scd[:, tc:tc + 1]),
                              waits=[(s_pe, mmd), (s_ve, s_ve.n)], inc=(s_se, 1))
                    bank_free[bank] = [(s_se, s1)]
                    vr = p.op("v", lambda e, par=par: e.tensor_tensor(
                        xio[par][:, :], xio[par][:, :], dn[par][:, :], ALU.add),
                        waits=[(s_se, s1), (s_lx[par], lx)], inc=(s_ve, 1))
                    sd = p.op("g", lambda e, tc=tc, ot2=ot2, par=par: e.dma_start(
                        out=xout[tc][:, ot2 * 512:(ot2 + 1) * 512],
                        in_=xio[par][:, :]),
                        waits=[(s_ve, vr)], inc=(s_sto[par], 16))
                    if xtra is not None:
                        sd = p.op("g", lambda e, tc=tc, ot2=ot2, par=par:
                                  e.dma_start(
                                      out=xtra[tc][:, ot2 * 512:(ot2 + 1) * 512],
                                      in_=xio[par][:, :]), inc=(s_sto[par], 16))
                    xio_free[par] = [(s_sto[par], sd)]
        return (s_sto[0].n, s_sto[1].n)

    def emit_attention(x_src_dram, start_waits):
        PB = 28 * KB
        _c = [PB]

        def sba(name, shape, dtype):
            off = (_c[0] + 31) // 32 * 32
            _c[0] = off + int(np.prod(shape[1:])) * mybir.dt.size(dtype)
            assert _c[0] <= 224 * KB, (name, _c[0])
            return sb(name, shape, dtype, off)

        xn2T_hi = sba("xn2T_hi", [128, 8, T], BF16)
        xn2T_lo = sba("xn2T_lo", [128, 8, T], BF16)
        latT_hi = sba("latT_hi", [128, 2, S], BF16)
        latT_lo = sba("latT_lo", [128, 2, S], BF16)
        KTs = [sba(f"KTs{i}", [128, S], BF16) for i in range(2)]
        V65 = [sba(f"V65_{i}", [128, 32, 65], BF16) for i in range(2)]
        Vlo = [sba(f"Vlo_{i}", [128, 32, 64], BF16) for i in range(2)]
        _qhid0_off = [(_c[0] + 31) // 32 * 32]
        QhiD = [sba(f"QhiD{i}", [128, T], BF16) for i in range(2)]
        _qlod_off = [(_c[0] + 31) // 32 * 32]
        QloD = [sba(f"QloD{i}", [128, T], BF16) for i in range(2)]
        pT_f = [sba(f"pT_f{i}", [128, 512], F32) for i in range(4)]
        pT_hi = [sba(f"pT_hi{i}", [128, 512], BF16) for i in range(6)]
        pT_lo = [sba(f"pT_lo{i}", [128, 512], BF16) for i in range(6)]
        rb_sb = sba("rb_sb", [64, 512], F32)
        o_scr = sba("o_scr", [64, 512], F32)
        oT_hi_s = [sba(f"oT_hi_s{i}", [64, 512], BF16) for i in range(2)]
        oT_lo_s = [sba(f"oT_lo_s{i}", [64, 512], BF16) for i in range(2)]
        wq_f = sba("wq_f", [128, 8, 128], F32)
        wk_f = sba("wk_f", [128, 2, 128], F32)
        wv_f = sba("wv_f", [128, 2, 128], F32)
        wv_hi = sba("wv_hi", [128, 2, 128], BF16)
        wv_lo = sba("wv_lo", [128, 2, 128], BF16)
        wl_hi = sb("wl_hi", [128, 8, 256], BF16, _qlod_off[0])
        wl_lo = sb("wl_lo", [128, 8, 256], BF16, _qlod_off[0] + 4 * KB)
        bqd = sba("bqd", [128, 16], F32)
        bkd = sba("bkd", [128, 16], F32)
        bl_sb = sba("bl_sb", [128, 2], F32)
        bv_row = sba("bv_row", [1, 128], F32)
        bv_bc = sba("bv_bc", [128, 128], F32)
        wqd_hi = [sba(f"wqd_hi{i}", [128, 8, 128], BF16) for i in range(2)]
        wqd_lo = [sba(f"wqd_lo{i}", [128, 8, 128], BF16) for i in range(2)]
        wkd_hi = [sba(f"wkd_hi{i}", [128, 2, 128], BF16) for i in range(2)]
        wkd_lo = [sba(f"wkd_lo{i}", [128, 2, 128], BF16) for i in range(2)]
        scr4k = sba("scr4k", [128, D], F32)
        xc2 = [scr4k]

        class _ColView:
            def __init__(self, t, c0, w):
                self.t, self.c0, self.w = t, c0, w

            def __getitem__(self, idx):
                ps, cs = idx
                a = self.c0 + (cs.start or 0)
                b = self.c0 + (cs.stop if cs.stop is not None else self.w)
                return self.t[ps, a:b]

        et0 = _ColView(scr4k, 0, 512)
        sqs = _ColView(scr4k, 512, 512)
        mqk = sba("mqk", [128, 8], F32)
        negb = [sba(f"negb{i}", [128, 1], F32) for i in range(2)]
        ones64 = sba("ones64", [128, 64], F32)
        idn_f = sba("idn_f", [128, 128], F32)
        wl_f = sb("wl_f", [128, 8, 256], F32, _qhid0_off[0])

        ldw = p.op("y", lambda e: e.dma_start(out=idn_f[:, :], in_=d_idf),
                   waits=list(start_waits), inc=(s_wo, 16))
        ldw = p.op("y", lambda e: e.dma_start(
            out=wl_f[:, :, :], in_=dwl.rearrange("(a p) l -> p a l", p=128)),
            inc=(s_wo, 16))
        for dst, dsrc in ((bqd, dbq), (bkd, dbk)):
            for lohalf in range(2):
                ldw = p.op("y", lambda e, dst=dst, dsrc=dsrc, lohalf=lohalf:
                           e.dma_start(
                               out=dst[lohalf * 64:(lohalf + 1) * 64, :],
                               in_=dsrc.rearrange("(a p) -> p a", p=64),
                               allow_slow_non_contiguous=True),
                           inc=(s_wo, 16))
        ldw = p.op("y", lambda e: e.dma_start(
            out=bl_sb[:, :], in_=dbl.rearrange("(a p) -> p a", p=128),
            allow_slow_non_contiguous=True),
            inc=(s_wo, 16))
        w_onoff = s_wo.n
        p.op("v", lambda e: e.memset(V65[0][:, :, 64:65], 1.0),
             waits=list(start_waits))
        memset(ones64[:, :], 1.0)
        memset(V65[1][:, :, 64:65], 1.0)
        p.op("v", lambda e: e.tensor_copy(wl_hi[:, :, :], wl_f[:, :, :]),
             waits=[(s_wo, w_onoff)])
        vwl = p.op("v", lambda e: e.scalar_tensor_tensor(
            wl_lo[:, :, :], wl_hi[:, :, :], -1.0, wl_f[:, :, :],
            ALU.mult, ALU.add))

        xin = x_src_dram.rearrange("(c p) d -> c p d", p=128)
        xfree = list(start_waits)
        for tc in range(NT):
            ld = p.op("y", lambda e, tc=tc: e.dma_start(
                out=xc2[0][:, :], in_=xin[tc]),
                waits=xfree, inc=(s_x2, 16))
            xb = xc2[0]
            p.op("v", lambda e, xb=xb: e.bn_stats(st6[:, 0:6], xb[:, 0:512]),
                 waits=[(s_x2, ld)])
            p.op("v", lambda e, xb=xb: e.bn_stats(st6[:, 6:12], xb[:, 512:1024]))
            v0 = p.op("v", lambda e: e.bn_aggr(mv2[:, :], st6[:, 0:12]))
            p.op("s", lambda e: e.activation(q_sc[:, 0:1], mv2[:, 1:2], AF.Ln,
                                             bias=c_eps[:, :]), waits=[(s_ve, v0)])
            s0 = p.op("s", lambda e: e.activation(q_sc[:, 1:2], q_sc[:, 0:1],
                                                  AF.Exp, scale=-0.5))
            vn = p.op("v", lambda e, xb=xb: e.tensor_scalar(
                xb[:, :], xb[:, :], mv2[:, 0:1], q_sc[:, 1:2],
                ALU.subtract, ALU.mult), waits=[(s_se, s0)])
            xfree = [(s_ve, vn)]
            for half in range(2):
                m = None
                for j in range(4):
                    ic = half * 4 + j
                    w = ([(s_ve, vn), (s_wo, w_onoff)] + bank_free[half]) \
                        if j == 0 else []
                    m = p.op("t", lambda e, half=half, j=j, ic=ic, xb=xb:
                             e.transpose(pb[half][:, j * 128:(j + 1) * 128],
                                         xb[:, ic * 128:(ic + 1) * 128],
                                         idn_f[:, :]),
                             waits=w, inc=((s_pe, 1) if j == 3 else None))
                p.op("v", lambda e, half=half, tc=tc: e.tensor_copy(
                    xn2T_hi[:, half * 4:(half + 1) * 4, tc * 128:(tc + 1) * 128],
                    pb[half][:, :]), waits=[(s_pe, m)])
                ev = p.op("v", lambda e, half=half, tc=tc: e.scalar_tensor_tensor(
                    xn2T_lo[:, half * 4:(half + 1) * 4, tc * 128:(tc + 1) * 128],
                    xn2T_hi[:, half * 4:(half + 1) * 4, tc * 128:(tc + 1) * 128],
                    -1.0, pb[half][:, :], ALU.mult, ALU.add))
                bank_free[half] = [(s_ve, ev)]
            xfree = [(s_ve, vn), (s_pe, s_pe.n)]
        xn2_done = s_ve.n

        terms = [(wl_hi, xn2T_hi), (wl_hi, xn2T_lo), (wl_lo, xn2T_hi)]
        for lc in range(2):
            for t4 in range(4):
                bank = 4 + (t4 % 2)
                n_mm = 0
                for whi, xn in terms:
                    for ic in range(8):
                        w = ([(s_ve, max(xn2_done, vwl))] + bank_free[bank]) \
                            if n_mm == 0 else []
                        p.op("t", lambda e, whi=whi, xn=xn, ic=ic, lc=lc, t4=t4,
                             bank=bank, st=(n_mm == 0), sp=(n_mm == 23): e.matmul(
                                 pb[bank][:, :], whi[:, ic, lc * 128:(lc + 1) * 128],
                                 xn[:, ic, t4 * 512:(t4 + 1) * 512],
                                 start=st, stop=sp),
                             waits=w, inc=((s_pe, 1) if n_mm == 23 else None))
                        n_mm += 1
                mml = s_pe.n
                s1 = p.op("s", lambda e, bank=bank, lc=lc: e.activation(
                    et0[:, :], pb[bank][:, :], AF.Identity, bias=bl_sb[:, lc:lc + 1]),
                    waits=[(s_pe, mml), (s_ve, s_ve.n)], inc=(s_se, 1))
                bank_free[bank] = [(s_se, s1)]
                p.op("v", lambda e, lc=lc, t4=t4: e.tensor_copy(
                    latT_hi[:, lc, t4 * 512:(t4 + 1) * 512], et0[:, :]),
                    waits=[(s_se, s1)])
                p.op("v", lambda e, lc=lc, t4=t4: e.scalar_tensor_tensor(
                    latT_lo[:, lc, t4 * 512:(t4 + 1) * 512],
                    latT_hi[:, lc, t4 * 512:(t4 + 1) * 512], -1.0, et0[:, :],
                    ALU.mult, ALU.add))
        lat_own = s_ve.n

        ccin = lat_cc_in.ap()
        for i_sec, (tsr, lc) in enumerate([(latT_hi, 0), (latT_hi, 1),
                                           (latT_lo, 0), (latT_lo, 1)]):
            p.op("y", lambda e, tsr=tsr, lc=lc, i_sec=i_sec: e.dma_start(
                out=ccin[i_sec * 128:(i_sec + 1) * 128, :],
                in_=tsr[:, lc, 0:T]),
                waits=[(s_ve, lat_own)] if i_sec == 0 else [], inc=(s_cin, 16))
        cin_done = s_cin.n
        cc = p.op("g", lambda e: e.collective_compute(
            "AllGather", ALU.bypass, replica_groups=[[0, 1], [2, 3], [4, 5], [6, 7]],
            ins=[lat_cc_in.ap().opt()], outs=[lat_cc_out.ap().opt()]),
            waits=[(s_cin, cin_done)], inc=(s_cc, 1))
        ccout = lat_cc_out.ap()
        for r in range(2):
            for i_sec, (tsr, lc) in enumerate([(latT_hi, 0), (latT_hi, 1),
                                               (latT_lo, 0), (latT_lo, 1)]):
                p.op("y", lambda e, r=r, tsr=tsr, lc=lc, i_sec=i_sec: e.dma_start(
                    out=tsr[:, lc, r * T:(r + 1) * T],
                    in_=ccout[r * 512 + i_sec * 128:r * 512 + (i_sec + 1) * 128, :]),
                    waits=[(s_cc, cc)] if (r == 0 and i_sec == 0) else [],
                    inc=(s_lat, 16))
        lat_full = s_lat.n

        wq_src = dwq.rearrange("(a p) o -> p a o", p=128)
        wk_src = dwk.rearrange("(a p) o -> p a o", p=128)
        wv_src = dwv.rearrange("(a p) o -> p a o", p=128)
        pair_w_done = {}
        split_done = {}
        pend_den = []
        sqs_free = []

        def flush_denominator():
            ob, hg, qt, mo = pend_den.pop(0)
            p.op("s", lambda e, ob=ob: e.activation(
                sqs[64:65, 0:512], pb[ob][64:65, :], AF.Ln),
                waits=[(s_pe, mo)] + list(sqs_free))
            vr1 = p.op("s", lambda e: e.activation(
                sqs[64:65, 0:512], sqs[64:65, 0:512], AF.Exp, scale=-1.0),
                inc=(s_se, 1))
            m_rb = p.op("t", lambda e: e.matmul(
                pb[6][0:64, 0:512], ones64[64:65, 0:64], sqs[64:65, 0:512],
                start=True, stop=True),
                waits=[(s_se, vr1)] + bank_free[6], inc=(s_pe, 1))
            sqs_free[:] = [(s_pe, m_rb)]
            p.op("v", lambda e: e.tensor_copy(rb_sb[:, :], pb[6][0:64, 0:512]),
                 waits=[(s_pe, m_rb)])
            bank_free[6] = [(s_ve, s_ve.n)]
            p.op("v", lambda e, ob=ob: e.tensor_tensor(
                o_scr[:, :], pb[ob][0:64, :], rb_sb[:, :], ALU.mult))
            spar = (hg * 4 + qt) % 2
            p.op("v", lambda e, spar=spar: e.tensor_copy(
                oT_hi_s[spar][:, :], o_scr[:, :]),
                waits=list(oT_free.get(spar, [])))
            vos = p.op("v", lambda e, spar=spar: e.scalar_tensor_tensor(
                oT_lo_s[spar][:, :], oT_hi_s[spar][:, :], -1.0,
                o_scr[:, :], ALU.mult, ALU.add))
            bank_free[ob] = [(s_ve, vos)]
            so = p.op("g", lambda e, hg=hg, qt=qt, spar=spar: e.dma_start(
                out=doT_hi[hg * 64:(hg + 1) * 64, qt * 512:(qt + 1) * 512],
                in_=oT_hi_s[spar][:, :]),
                waits=[(s_ve, vos)], inc=(s_ot[spar], 16))
            so = p.op("g", lambda e, hg=hg, qt=qt, spar=spar: e.dma_start(
                out=doT_lo[hg * 64:(hg + 1) * 64, qt * 512:(qt + 1) * 512],
                in_=oT_lo_s[spar][:, :]), inc=(s_ot[spar], 16))
            oT_free[spar] = [(s_ot[spar], so)]

        def load_pair_slices(pr):
            w = ([(s_ve, split_done[pr - 1]), (s_pe, s_pe.n)]
                 if pr >= 1 else [])
            p.op("y", lambda e, pr=pr: e.dma_start(
                out=wq_f[:, :, :], in_=wq_src[:, :, pr * 128:(pr + 1) * 128]),
                waits=w, inc=(s_wp, 16))
            p.op("y", lambda e, pr=pr: e.dma_start(
                out=wk_f[:, :, :], in_=wk_src[:, :, pr * 128:(pr + 1) * 128]),
                inc=(s_wp, 16))
            p.op("y", lambda e, pr=pr: e.dma_start(
                out=wv_f[:, :, :], in_=wv_src[:, :, pr * 128:(pr + 1) * 128]),
                inc=(s_wp, 16))
            p.op("y", lambda e, pr=pr: e.dma_start(
                out=bv_row[0:1, :], in_=dbv[None, pr * 128:(pr + 1) * 128]),
                inc=(s_wp, 16))
            pair_w_done[pr] = s_wp.n

        load_pair_slices(0)

        for pr in range(8):
            wsp = [(s_wp, pair_w_done[pr]), (s_pe, s_pe.n)]
            first = True
            for hh in range(2):
                for half in range(2):
                    a, b2 = half * 64, (half + 1) * 64
                    p.op("v", lambda e, hh=hh, a=a, b2=b2: e.tensor_copy(
                        wqd_hi[hh][:, :, a:b2],
                        wq_f[:, :, hh * 64:(hh + 1) * 64]),
                        waits=(wsp if first else []))
                    first = False
                    p.op("v", lambda e, hh=hh, a=a, b2=b2: e.scalar_tensor_tensor(
                        wqd_lo[hh][:, :, a:b2], wqd_hi[hh][:, :, a:b2], -1.0,
                        wq_f[:, :, hh * 64:(hh + 1) * 64], ALU.mult, ALU.add))
                    p.op("v", lambda e, hh=hh, a=a, b2=b2: e.tensor_copy(
                        wkd_hi[hh][:, :, a:b2],
                        wk_f[:, :, hh * 64:(hh + 1) * 64]))
                    p.op("v", lambda e, hh=hh, a=a, b2=b2: e.scalar_tensor_tensor(
                        wkd_lo[hh][:, :, a:b2], wkd_hi[hh][:, :, a:b2], -1.0,
                        wk_f[:, :, hh * 64:(hh + 1) * 64], ALU.mult, ALU.add))
            p.op("v", lambda e: e.tensor_copy(
                wv_hi[:, :, :], wv_f[:, :, :]))
            sd = p.op("v", lambda e: e.scalar_tensor_tensor(
                wv_lo[:, :, :], wv_hi[:, :, :], -1.0, wv_f[:, :, :],
                ALU.mult, ALU.add))
            m_bv = p.op("t", lambda e: e.matmul(
                pb[7][:, 0:128], ones_row[:, :], bv_row[0:1, :],
                start=True, stop=True),
                waits=[(s_ve, sd)] + bank_free[7], inc=(s_pe, 1))
            split_done[pr] = sd
            if pr + 1 < 8:
                load_pair_slices(pr + 1)
            vbv = p.op("v", lambda e: e.tensor_copy(bv_bc[:, :], pb[7][:, 0:128]),
                       waits=[(s_pe, m_bv)])
            bank_free[7] = [(s_ve, vbv)]

            for hh in range(2):
                kterms = [(wkd_hi[hh], latT_hi), (wkd_hi[hh], latT_lo),
                          (wkd_lo[hh], latT_hi)]
                for kt in range(8):
                    bank = 4 + (kt % 2)
                    n_mm = 0
                    for wsl_, lat in kterms:
                        for lc2 in range(2):
                            w = ([(s_ve, sd), (s_lat, lat_full)] +
                                 bank_free[bank]) if n_mm == 0 else []
                            p.op("t", lambda e, wsl_=wsl_, lat=lat, lc2=lc2,
                                 kt=kt, bank=bank, st=(n_mm == 0),
                                 sp=(n_mm == 5): e.matmul(
                                     pb[bank][:, :], wsl_[:, lc2, :],
                                     lat[:, lc2, kt * 512:(kt + 1) * 512],
                                     start=st, stop=sp),
                                 waits=w, inc=((s_pe, 1) if n_mm == 5 else None))
                            n_mm += 1
                    mmk = s_pe.n
                    s1 = p.op("s", lambda e, bank=bank, pr=pr, hh=hh:
                              e.activation(et0[:, :], pb[bank][:, :],
                                           AF.Identity,
                                           bias=bkd[:, 2 * pr + hh:2 * pr + hh + 1]),
                              waits=[(s_pe, mmk), (s_ve, s_ve.n)], inc=(s_se, 1))
                    bank_free[bank] = [(s_se, s1)]
                    p.op("v", lambda e, hh=hh, kt=kt: e.tensor_copy(
                        KTs[hh][:, kt * 512:(kt + 1) * 512], et0[:, :]),
                        waits=[(s_se, s1)])
                    p.op("v", lambda e, hh=hh, kt=kt: e.scalar_tensor_tensor(
                        KTs[hh][64:128, kt * 512:(kt + 1) * 512],
                        KTs[hh][64:128, kt * 512:(kt + 1) * 512], -1.0,
                        et0[64:128, :], ALU.mult, ALU.add))
            vterms = [(wv_hi, latT_hi), (wv_hi, latT_lo), (wv_lo, latT_hi)]
            for kc4 in range(8):
                bank = 4 + (kc4 % 2)
                for qtr in range(4):
                    kc = kc4 * 4 + qtr
                    n_mm = 0
                    for wsl_, lat in vterms:
                        for lc2 in range(2):
                            w = (bank_free[bank] if (qtr == 0 and n_mm == 0)
                                 else [])
                            p.op("t", lambda e, wsl_=wsl_, lat=lat, lc2=lc2,
                                 kc=kc, bank=bank, qtr=qtr, st=(n_mm == 0),
                                 sp=(n_mm == 5): e.matmul(
                                     pb[bank][:, qtr * 128:(qtr + 1) * 128],
                                     lat[:, lc2, kc * 128:(kc + 1) * 128],
                                     wsl_[:, lc2, :], start=st, stop=sp),
                                 waits=w, inc=((s_pe, 1) if n_mm == 5 else None))
                            n_mm += 1
                mmv = s_pe.n
                for qtr in range(4):
                    kc = kc4 * 4 + qtr
                    vv = p.op("v", lambda e, bank=bank, qtr=qtr: e.tensor_tensor(
                        et0[:, 0:128], pb[bank][:, qtr * 128:(qtr + 1) * 128],
                        bv_bc[:, :], ALU.add),
                        waits=[(s_pe, mmv), (s_ve, vbv)] if qtr == 0 else
                        [(s_pe, mmv)])
                    for hh in range(2):
                        p.op("v", lambda e, hh=hh, kc=kc: e.tensor_copy(
                            V65[hh][:, kc, 0:64],
                            et0[:, hh * 64:(hh + 1) * 64]))
                        p.op("v", lambda e, hh=hh, kc=kc: e.scalar_tensor_tensor(
                            Vlo[hh][:, kc, 0:64], V65[hh][:, kc, 0:64], -1.0,
                            et0[:, hh * 64:(hh + 1) * 64], ALU.mult, ALU.add))
                bank_free[bank] = [(s_ve, s_ve.n)]

            for hh in range(2):
                qterms = [(wqd_hi[hh], xn2T_hi), (wqd_hi[hh], xn2T_lo),
                          (wqd_lo[hh], xn2T_hi)]
                for t4 in range(4):
                    bank = 4 + (t4 % 2)
                    n_mm = 0
                    for wsl_, xn in qterms:
                        for ic in range(8):
                            w = ([(s_ve, sd)] + bank_free[bank]) \
                                if n_mm == 0 else []
                            p.op("t", lambda e, wsl_=wsl_, xn=xn, ic=ic, t4=t4,
                                 bank=bank, st=(n_mm == 0), sp=(n_mm == 23):
                                 e.matmul(
                                     pb[bank][:, :], wsl_[:, ic, :],
                                     xn[:, ic, t4 * 512:(t4 + 1) * 512],
                                     start=st, stop=sp),
                                 waits=w, inc=((s_pe, 1) if n_mm == 23 else None))
                            n_mm += 1
                    mmq = s_pe.n
                    s1 = p.op("s", lambda e, bank=bank, pr=pr, hh=hh:
                              e.activation(et0[:, :], pb[bank][:, :],
                                           AF.Identity,
                                           bias=bqd[:, 2 * pr + hh:2 * pr + hh + 1]),
                              waits=[(s_pe, mmq), (s_ve, s_ve.n)], inc=(s_se, 1))
                    bank_free[bank] = [(s_se, s1)]
                    p.op("v", lambda e, hh=hh, t4=t4: e.tensor_copy(
                        QhiD[hh][:, t4 * 512:(t4 + 1) * 512], et0[:, :]),
                        waits=[(s_se, s1)])
                    p.op("v", lambda e, hh=hh, t4=t4: e.scalar_tensor_tensor(
                        QloD[hh][:, t4 * 512:(t4 + 1) * 512],
                        QhiD[hh][:, t4 * 512:(t4 + 1) * 512], -1.0, et0[:, :],
                        ALU.mult, ALU.add))
            sq_bufs = (et0, sqs)
            sq_free = {0: [], 1: []}
            tmp_col = (6, 7)
            seg_i = 0
            first_norm = [(s_pe, s_pe.n), (s_ve, s_ve.n)]
            for hh in range(2):
                for src_t, seg, col, width in ((KTs[hh], 8, hh, 512),
                                               (QhiD[hh], 4, 3 + hh, 512)):
                    for kt in range(seg):
                        par = seg_i % 2
                        bnk = 6 + par
                        s2 = p.op("s", lambda e, src_t=src_t, kt=kt, width=width,
                                  par=par: e.activation(
                                      sq_bufs[par][:, 0:width],
                                      src_t[:, kt * width:(kt + 1) * width],
                                      AF.Square),
                                  waits=first_norm + sq_free[par],
                                  inc=(s_se, 1))
                        first_norm = []
                        m_n = p.op("t", lambda e, width=width, par=par, bnk=bnk:
                                   e.matmul(
                                       pb[bnk][0:1, 0:width], ones_col[:, :],
                                       sq_bufs[par][:, 0:width],
                                       start=True, stop=True),
                                   waits=[(s_se, s2)] + bank_free[bnk],
                                   inc=(s_pe, 1))
                        sq_free[par] = [(s_pe, m_n)]
                        p.op("v", lambda e, width=width, par=par, bnk=bnk:
                             e.reduce_max(
                                 mqk[0:1, tmp_col[par]:tmp_col[par] + 1],
                                 pb[bnk][0:1, 0:width], axis=AXL.X),
                             waits=[(s_pe, m_n)])
                        vkn = p.op("v", lambda e, col=col, kt=kt, par=par: (
                            e.tensor_copy(mqk[0:1, col:col + 1],
                                          mqk[0:1, tmp_col[par]:tmp_col[par] + 1])
                            if kt == 0 else
                            e.tensor_tensor(mqk[0:1, col:col + 1],
                                            mqk[0:1, col:col + 1],
                                            mqk[0:1, tmp_col[par]:tmp_col[par] + 1],
                                            ALU.max)))
                        bank_free[bnk] = [(s_ve, vkn)]
                        seg_i += 1

                p.op("v", lambda e, hh=hh: e.tensor_tensor(
                    mqk[0:1, 5:6], mqk[0:1, hh:hh + 1], mqk[0:1, 3 + hh:4 + hh],
                    ALU.mult))
                p.op("s", lambda e: e.activation(
                    mqk[0:1, 2:3], mqk[0:1, 5:6], AF.Ln),
                    waits=[(s_ve, s_ve.n)])
                p.op("s", lambda e: e.activation(
                    mqk[0:1, 2:3], mqk[0:1, 2:3], AF.Exp, scale=0.5))
                vb = p.op("v", lambda e: e.tensor_single_scalar(
                    mqk[0:1, 5:6], mqk[0:1, 2:3],
                    -0.0625 * 1.002, ALU.mult),
                    waits=[(s_se, s_se.n)])
                m_b = p.op("t", lambda e: e.matmul(
                    pb[6][:, 0:1], ones_row[:, :], mqk[0:1, 5:6],
                    start=True, stop=True),
                    waits=[(s_ve, vb)] + bank_free[6], inc=(s_pe, 1))
                vnb = p.op("v", lambda e, hh=hh: e.tensor_copy(
                    negb[hh][:, :], pb[6][:, 0:1]), waits=[(s_pe, m_b)])
                bank_free[6] = [(s_ve, vnb)]

            for hh in range(2):
                hg = pr * 2 + hh
                for qg in range(2):
                    qts = (2 * qg, 2 * qg + 1)
                    iq = (hh * 2 + qg) % 2
                    obs = (2, 3) if iq == 0 else (4, 5)
                    exp_done = {}
                    split_v = {}
                    split_g = {}
                    av_done = {}

                    def emit_scg(kc, hh=hh, qts=qts):
                        for qi, qt in enumerate(qts):
                            sbk = ((0, 1) if kc % 2 == 0 else (6, 7))[qi]
                            w = list(bank_free[sbk]) if True else []
                            p.op("t", lambda e, hh=hh, kc=kc, qt=qt, sbk=sbk:
                                 e.matmul(pb[sbk][:, :],
                                          KTs[hh][:, kc * 128:(kc + 1) * 128],
                                          QhiD[hh][:, qt * 512:(qt + 1) * 512],
                                          start=True, stop=False),
                                 waits=w)
                            msc = p.op("t", lambda e, hh=hh, kc=kc, qt=qt,
                                       sbk=sbk: e.matmul(
                                           pb[sbk][:, :],
                                           KTs[hh][:, kc * 128:(kc + 1) * 128],
                                           QloD[hh][:, qt * 512:(qt + 1) * 512],
                                           start=False, stop=True),
                                       inc=(s_pe, 1))
                            slot = (kc % 2) * 2 + qi
                            se_ = p.op("s", lambda e, slot=slot, sbk=sbk, hh=hh:
                                       e.activation(
                                           pT_f[slot][:, :], pb[sbk][:, :],
                                           AF.Exp, bias=negb[hh][:, :],
                                           scale=0.125),
                                       waits=[(s_pe, msc)] +
                                       split_v.get((kc - 2, qi), []) +
                                       split_g.get((kc - 2, qi), []),
                                       inc=(s_se, 1))
                            bank_free[sbk] = [(s_se, se_)]
                            exp_done[(kc, qi)] = se_

                    def emit_splitg(kc):
                        for qi in range(2):
                            fsl = (kc % 2) * 2 + qi
                            slot = (kc % 3) * 2 + qi
                            if qi == 0:
                                vh = p.op("v", lambda e, slot=slot, fsl=fsl:
                                          e.tensor_copy(
                                              pT_hi[slot][:, :], pT_f[fsl][:, :]),
                                          waits=[(s_se, exp_done[(kc, qi)])] +
                                          av_done.get(kc - 3, []))
                                gl = p.op("v", lambda e, slot=slot, fsl=fsl:
                                          e.scalar_tensor_tensor(
                                              pT_lo[slot][:, :],
                                              pT_hi[slot][:, :], -1.0,
                                              pT_f[fsl][:, :],
                                              ALU.mult, ALU.add))
                                split_v[(kc, qi)] = [(s_ve, vh)]
                                split_g[(kc, qi)] = [(s_ve, gl)]
                            else:
                                vh = p.op("s", lambda e, slot=slot, fsl=fsl:
                                          e.activation(
                                              pT_hi[slot][:, :], pT_f[fsl][:, :],
                                              AF.Copy, bias=0.0, scale=1.0),
                                          waits=av_done.get(kc - 3, []))
                                gl = p.op("g", lambda e, slot=slot, fsl=fsl:
                                          e.tensor_tensor(
                                              pT_lo[slot][:, :], pT_f[fsl][:, :],
                                              pT_hi[slot][:, :], ALU.subtract),
                                          waits=[(s_se, vh)] +
                                          av_done.get(kc - 3, []))
                                split_v[(kc, qi)] = [(s_se, vh)]
                                split_g[(kc, qi)] = [(s_gc, gl)]

                    def emit_avg(kc, hh=hh, qts=qts, obs=obs):
                        avw = (split_v[(kc, 0)] + split_g[(kc, 0)] +
                               split_v[(kc, 1)] + split_g[(kc, 1)])
                        if kc == 0:
                            avw = avw + bank_free[obs[0]] + bank_free[obs[1]]
                        slots = ((kc % 3) * 2, (kc % 3) * 2 + 1)
                        mav = None
                        if kc < 31:
                            for qi in range(2):
                                p.op("t", lambda e, hh=hh, kc=kc, s_=slots[qi],
                                     ob=obs[qi], st=(kc == 0): e.matmul(
                                         pb[ob][0:65, :], V65[hh][:, kc, 0:65],
                                         pT_hi[s_][:, :], start=st, stop=False),
                                     waits=(avw if qi == 0 else []))
                            for qi in range(2):
                                p.op("t", lambda e, hh=hh, kc=kc, s_=slots[qi],
                                     ob=obs[qi]: e.matmul(
                                         pb[ob][0:65, :], V65[hh][:, kc, 0:65],
                                         pT_lo[s_][:, :], start=False,
                                         stop=False))
                            for qi in range(2):
                                mav = p.op("t", lambda e, hh=hh, kc=kc,
                                           s_=slots[qi], ob=obs[qi]: e.matmul(
                                               pb[ob][0:64, :],
                                               Vlo[hh][:, kc, 0:64],
                                               pT_hi[s_][:, :], start=False,
                                               stop=False),
                                           inc=((s_pe, 1) if qi == 1 else None))
                        else:
                            for qi in range(2):
                                p.op("t", lambda e, hh=hh, kc=kc, s_=slots[qi],
                                     ob=obs[qi]: e.matmul(
                                         pb[ob][0:64, :], Vlo[hh][:, kc, 0:64],
                                         pT_hi[s_][:, :], start=False,
                                         stop=False),
                                     waits=(avw if qi == 0 else []))
                            for qi in range(2):
                                p.op("t", lambda e, hh=hh, kc=kc, s_=slots[qi],
                                     ob=obs[qi]: e.matmul(
                                         pb[ob][0:65, :], V65[hh][:, kc, 0:65],
                                         pT_hi[s_][:, :], start=False,
                                         stop=False))
                            for qi in range(2):
                                mav = p.op("t", lambda e, hh=hh, kc=kc,
                                           s_=slots[qi], ob=obs[qi]: e.matmul(
                                               pb[ob][0:65, :],
                                               V65[hh][:, kc, 0:65],
                                               pT_lo[s_][:, :], start=False,
                                               stop=True),
                                           inc=((s_pe, 1) if qi == 1 else None))
                        av_done[kc] = [(s_pe, mav)]

                    emit_scg(0)
                    emit_scg(1)
                    emit_splitg(0)
                    emit_splitg(1)
                    for kc in range(32):
                        if kc + 2 < 32:
                            emit_scg(kc + 2)
                        if kc == 4:
                            while pend_den:
                                flush_denominator()
                        emit_avg(kc)
                        if kc + 2 < 32:
                            emit_splitg(kc + 2)
                    mo = s_pe.n
                    pend_den.append((obs[0], hg, 2 * qg, mo))
                    pend_den.append((obs[1], hg, 2 * qg + 1, mo))
            while pend_den:
                flush_denominator()
        return (s_ot[0].n, s_ot[1].n)

    oT_free = {}

    def emit_wo_resid(ot_total, x_res_dram, x_out_dram, y_out, do_ln3, xqT_out):
        ot_waits = [(s_ot[0], ot_total[0]), (s_ot[1], ot_total[1])]
        """out = x_res + O @ Wo + bo, chunk by chunk; optionally LN3+quant
        into xqT_out. Writes x_out_dram (and y_out if not None)."""
        PB = 60 * KB
        _c = [PB]

        def sbw(name, shape, dtype):
            off = (_c[0] + 31) // 32 * 32
            _c[0] = off + int(np.prod(shape[1:])) * mybir.dt.size(dtype)
            assert _c[0] <= 160 * KB, (name, _c[0])
            return sb(name, shape, dtype, off)

        wo_f = sbw("wo_f", [128, 8, D], F32)
        wo_hi = sbw("wo_hi", [128, 8, D], BF16)
        wo_lo = sbw("wo_lo", [128, 8, D], BF16)
        qt3 = ([sb("qt3_f0", [128, D], F32, PB),
                sb("qt3_f1", [128, D], F32, PB + 4 * KB)],
               sb("qt3_i", [128, D], I32, PB + 8 * KB),
               sb("qt3_e", [128, D], F32, PB + 12 * KB),
               sb("qt3_c", [128, D], F32, PB + 16 * KB),
               sb("xq3_tm", [128, D], BF16, PB + 20 * KB),
               [sb("qt3_b0", [128, D], F32, PB + 22 * KB),
                sb("qt3_b1", [128, D], F32, PB + 26 * KB)])
        oT_b = [sbw(f"oT_b{i}", [128, 8, 128], BF16) for i in range(4)]
        bo_row = sbw("bo_row", [1, D], F32)
        bo_bc = sbw("bo_bc", [128, D], F32)
        x3 = [sbw(f"x3_{i}", [128, D], F32) for i in range(2)]
        dn3 = sbw("dn3", [128, D], F32)

        lwo = p.op("y", lambda e: e.dma_start(
            out=wo_f[:, :, :], in_=dwo.rearrange("(a p) o -> p a o", p=128)),
            waits=ot_waits, inc=(s_wo, 16))
        lbo = p.op("y", lambda e: e.dma_start(out=bo_row[0:1, :], in_=dbo[None, :]),
                   inc=(s_wo, 16))
        wo_done = s_wo.n
        p.op("v", lambda e: e.tensor_copy(wo_hi[:, :, :], wo_f[:, :, :]),
             waits=[(s_wo, wo_done)])
        vwo = p.op("v", lambda e: e.scalar_tensor_tensor(
            wo_lo[:, :, :], wo_hi[:, :, :], -1.0, wo_f[:, :, :],
            ALU.mult, ALU.add))
        for half in range(2):
            m_b = p.op("t", lambda e, half=half: e.matmul(
                pb[6][:, :], ones_row[:, :],
                bo_row[0:1, half * 512:(half + 1) * 512],
                start=True, stop=True),
                waits=[(s_wo, wo_done)] + bank_free[6], inc=(s_pe, 1))
            vb = p.op("v", lambda e, half=half: e.tensor_copy(
                bo_bc[:, half * 512:(half + 1) * 512], pb[6][:, :]),
                waits=[(s_pe, m_b)])
            bank_free[6] = [(s_ve, vb)]

        xin = x_res_dram.rearrange("(c p) d -> c p d", p=128)
        xout = x_out_dram.rearrange("(c p) d -> c p d", p=128)
        yout = (y_out.rearrange("(c p) d -> c p d", p=128)
                if y_out is not None else None)
        oT_hi_src = doT_hi.rearrange("(a p) t -> p a t", p=128)
        oT_lo_src = doT_lo.rearrange("(a p) t -> p a t", p=128)
        o_ld = {}
        x_ld = {}
        xq_ev3 = []
        pend_ln3 = None
        qtb_free3 = {0: 0, 1: 0}
        for tc in range(NT):
            par = tc % 2
            w = o_ld.get(tc - 2, [])
            l1 = p.op("y", lambda e, tc=tc, par=par: e.dma_start(
                out=oT_b[par][:, :, :],
                in_=oT_hi_src[:, :, tc * 128:(tc + 1) * 128]),
                waits=w + (ot_waits if tc < 2 else []),
                inc=(s_o3[par], 16))
            l2 = p.op("y", lambda e, tc=tc, par=par: e.dma_start(
                out=oT_b[2 + par][:, :, :],
                in_=oT_lo_src[:, :, tc * 128:(tc + 1) * 128]),
                inc=(s_o3[par], 16))
            lx = p.op("y", lambda e, tc=tc, par=par: e.dma_start(
                out=x3[par][:, :], in_=xin[tc]),
                waits=x_ld.get(tc - 2, []), inc=(s_lx[par], 16))
            oterms = [(0, wo_hi), (0, wo_lo), (2, wo_hi)]
            mm_end = {}
            for half in range(2):
                bank = 4 + (half % 2)
                n_mm = 0
                for obi, wo_ in oterms:
                    for dc in range(8):
                        w2 = []
                        if n_mm == 0:
                            w2 = ([(s_o3[par], l2), (s_ve, vwo)] +
                                  bank_free[bank])
                        p.op("t", lambda e, obi=obi, wo_=wo_, dc=dc, par=par,
                             half=half, bank=bank, st=(n_mm == 0),
                             sp=(n_mm == 23): e.matmul(
                                 pb[bank][:, :], oT_b[obi + par][:, dc, :],
                                 wo_[:, dc, half * 512:(half + 1) * 512],
                                 start=st, stop=sp),
                             waits=w2, inc=((s_pe, 1) if n_mm == 23 else None))
                        n_mm += 1
                mm_end[half] = s_pe.n
                vv = p.op("v", lambda e, bank=bank, half=half: e.tensor_tensor(
                    dn3[:, half * 512:(half + 1) * 512], pb[bank][:, :],
                    bo_bc[:, half * 512:(half + 1) * 512], ALU.add),
                    waits=[(s_pe, mm_end[half])])
                vv = p.op("v", lambda e, par=par, half=half: e.tensor_tensor(
                    x3[par][:, half * 512:(half + 1) * 512],
                    x3[par][:, half * 512:(half + 1) * 512],
                    dn3[:, half * 512:(half + 1) * 512], ALU.add),
                    waits=[(s_lx[par], lx)])
                bank_free[bank] = [(s_ve, vv)]
            o_ld[tc] = [(s_ve, s_ve.n)]
            st1 = p.op("g", lambda e, tc=tc, par=par: e.dma_start(
                out=xout[tc], in_=x3[par][:, :]),
                waits=[(s_ve, s_ve.n)], inc=(s_sto[par], 16))
            if yout is not None:
                st1 = p.op("g", lambda e, tc=tc, par=par: e.dma_start(
                    out=yout[tc], in_=x3[par][:, :]), inc=(s_sto[par], 16))
            x_ld[tc] = [(s_sto[par], st1)]
            if do_ln3:
                vm, sc2 = ln_quant_p1(x3[par], invact1[:, tc:tc + 1], tc,
                                      ow_waits=[(s_sto[par], st1)],
                                      scratch=qt3, qtb_free=qtb_free3)
                if pend_ln3 is not None:
                    _, ev = ln_quant_p2(xqT_out, pend_ln3[0], pend_ln3[0],
                                        pend_ln3[1], scratch=qt3,
                                        qtb_free=qtb_free3)
                    xq_ev3.append(ev)
                pend_ln3 = (tc, sc2)
                x_ld[tc] = [(s_ve, vm)]
        if pend_ln3 is not None:
            _, ev = ln_quant_p2(xqT_out, pend_ln3[0], pend_ln3[0], pend_ln3[1],
                                scratch=qt3, qtb_free=qtb_free3)
            xq_ev3.append(ev)
        return (s_sto[0].n, s_sto[1].n, s_pe.n, xq_ev3)

    p.op("v", lambda e: e.tensor_single_scalar(scg_all[:, :], invact1[:, :],
                                               iws_g[:, :], ALU.mult),
         waits=[(s_ve, max(ws_ready, xq_done))])
    vs = p.op("v", lambda e: e.tensor_single_scalar(scu_all[:, :], invact1[:, :],
                                                    iws_u[:, :], ALU.mult),
              inc=(s_ve, 1))

    if STAGE == 1:
        f0, f1 = emit_ffn(xq1T, dx, dx_cur, dy, scg_all, scu_all, scd_all,
                          [(s_ve, xq_done)], xq_ev1,
                          dq2T[0], sc_gate=[(s_ve, vs)], wst=prep_st,
                          gu_gate=[(s_se, prep_s_last), prep_v_last] +
                          prep_done_gd,
                          ev_gate=list(prep_done_gd))
        p.op("g", lambda e: e.nop(), waits=[(s_sto[0], f0), (s_sto[1], f1)])
        p.op("y", lambda e: e.nop(), waits=[(s_sto[0], f0), (s_sto[1], f1)])
    elif STAGE == 2:
        f0, f1 = emit_ffn(xq1T, dx, dx_cur, None, scg_all, scu_all, scd_all,
                          [(s_ve, xq_done)], xq_ev1,
                          dq2T[0], sc_gate=[(s_ve, vs)], wst=prep_st,
                          gu_gate=[(s_se, prep_s_last), prep_v_last] +
                          prep_done_gd,
                          ev_gate=list(prep_done_gd))
        ot_total = emit_attention(dx_cur, [(s_sto[0], f0), (s_sto[1], f1)])
        g0, g1, _, _ = emit_wo_resid(ot_total, dx_cur, dx_cur2, dy, False, None)
        p.op("g", lambda e: e.nop(), waits=[(s_sto[0], g0), (s_sto[1], g1)])
        p.op("y", lambda e: e.nop(), waits=[(s_sto[0], g0), (s_sto[1], g1)])
    elif STAGE == 3:
        f0, f1 = emit_ffn(xq1T, dx, dx_cur, None, scg_all, scu_all, scd_all,
                          [(s_ve, xq_done)], xq_ev1,
                          dq2T[0], sc_gate=[(s_ve, vs)], wst=prep_st,
                          gu_gate=[(s_se, prep_s_last), prep_v_last] +
                          prep_done_gd,
                          ev_gate=list(prep_done_gd))
        ot_total = emit_attention(dx_cur, [(s_sto[0], f0), (s_sto[1], f1)])
        g0, g1, p3_pe, xq_ev3 = emit_wo_resid(ot_total, dx_cur, dx_cur2, None,
                                              True, xq1T)
        p.op("v", lambda e: e.tensor_single_scalar(
            scg_all[:, :], invact1[:, :], iws_g[:, :], ALU.mult))
        vs2 = p.op("v", lambda e: e.tensor_single_scalar(
            scu_all[:, :], invact1[:, :], iws_u[:, :], ALU.mult))
        h0, h1 = emit_ffn(xq1T, dx_cur2, dy, None, scg_all, scu_all, scd_all,
                          [(s_ot[0], ot_total[0]), (s_ot[1], ot_total[1])],
                          xq_ev3, dq2T[1], sc_gate=[(s_ve, vs2)],
                          gu_gate=[(s_pe, p3_pe), (s_ve, xq_ev3[NT - 1])],
                          ev_gate=[(s_pe, p3_pe), (s_ve, xq_ev3[NT - 1])])
        p.op("g", lambda e: e.nop(), waits=[(s_sto[0], h0), (s_sto[1], h1)])
        p.op("y", lambda e: e.nop(), waits=[(s_sto[0], h0), (s_sto[1], h1)])

    p.emit()
    return nc


def kernel(L0_attn_x, L0_attn_Wq, L0_attn_bq, L0_attn_Wl, L0_attn_bl,
           L0_attn_Wk, L0_attn_bk, L0_attn_Wv, L0_attn_bv, L0_attn_Wo,
           L0_attn_bo, L0_ffn_w_gate, L0_ffn_w_up, L0_ffn_w_down):
    x = np.ascontiguousarray(np.asarray(L0_attn_x, dtype=np.float32))
    assert x.shape == (4, 4096, 1024)
    idn = np.eye(128)
    in_maps = []
    for c in range(N_CORES):
        b, h = c // 2, c % 2
        m = {
            "x": np.ascontiguousarray(x[b, h * T:(h + 1) * T, :]),
            "w_gate": np.ascontiguousarray(np.asarray(L0_ffn_w_gate, np.float32)),
            "w_up": np.ascontiguousarray(np.asarray(L0_ffn_w_up, np.float32)),
            "w_down": np.ascontiguousarray(np.asarray(L0_ffn_w_down, np.float32)),
            "idn_bf16": idn.astype(ml_dtypes.bfloat16),
        }
        if STAGE >= 2:
            m.update({
                "Wq": np.ascontiguousarray(np.asarray(L0_attn_Wq, np.float32)),
                "Wl": np.ascontiguousarray(np.asarray(L0_attn_Wl, np.float32)),
                "Wk": np.ascontiguousarray(np.asarray(L0_attn_Wk, np.float32)),
                "Wv": np.ascontiguousarray(np.asarray(L0_attn_Wv, np.float32)),
                "Wo": np.ascontiguousarray(np.asarray(L0_attn_Wo, np.float32)),
                "bq": np.ascontiguousarray(np.asarray(L0_attn_bq, np.float32)),
                "bl": np.ascontiguousarray(np.asarray(L0_attn_bl, np.float32)),
                "bk": np.ascontiguousarray(np.asarray(L0_attn_bk, np.float32)),
                "bv": np.ascontiguousarray(np.asarray(L0_attn_bv, np.float32)),
                "bo": np.ascontiguousarray(np.asarray(L0_attn_bo, np.float32)),
                "idn_f32": idn.astype(np.float32),
            })
        in_maps.append(m)
    nc = build_nc()
    res = run_bass_kernel_spmd(nc, in_maps, core_ids=list(range(N_CORES)))
    out = np.empty((4, 4096, 1024), dtype=np.float32)
    for c in range(N_CORES):
        b, h = c // 2, c % 2
        out[b, h * T:(h + 1) * T, :] = res.results[c]["y"]
    return out


# version 16
# speedup vs baseline: 1.1645x; 1.0012x over previous
import numpy as np
import ml_dtypes
import concourse.bass as bass
import concourse.mybir as mybir
from concourse.bass_utils import run_bass_kernel_spmd

F32 = mybir.dt.float32
BF16 = mybir.dt.bfloat16
FP16 = mybir.dt.float16
I32 = mybir.dt.int32
AF = mybir.ActivationFunctionType
ALU = mybir.AluOpType
AXL = mybir.AxisListType

STAGE = 3

D = 1024
INNER = 4096
H = 16
DH = 64
LAT = 256
T = 2048
S = 4096
NT = T // 128
N_CORES = 8
TB = 512
NTB = T // TB
EPS = 1e-5
KB = 1024


class Sem:
    def __init__(self, handle, name):
        self.h, self.name, self.n = handle, name, 0


class Prog:

    ENGINES = ("t", "v", "s", "g", "y")

    def __init__(self, nc):
        self.nc = nc
        self.q = {e: [] for e in self.ENGINES}
        self._cms = []
        self.auto = {}

    def sem(self, name):
        cm = self.nc.semaphore(name)
        h = cm.__enter__()
        self._cms.append(cm)
        return Sem(h, name)

    def op(self, eng, build, waits=(), inc=None):
        own = self.auto.get(eng)
        waits = [(s, v) for (s, v) in waits if v > 0]
        if own is not None:
            if own.n > 0 and not any(s is own and v >= own.n for s, v in waits):
                waits.append((own, own.n))
            if inc is None:
                inc = (own, 1)
        if inc is not None:
            inc[0].n += inc[1]
        self.q[eng].append((build, waits, inc))
        return None if inc is None else inc[0].n

    def emit(self):
        with self.nc.Block() as block:
            sections = {"t": block.tensor, "v": block.vector, "s": block.scalar,
                        "g": block.gpsimd, "y": block.sync}
            for key in self.ENGINES:
                ops = self.q[key]
                if not ops:
                    continue

                def body(e, ops=ops):
                    for build, waits, inc in ops:
                        for sem, v in waits:
                            e.wait_ge(sem.h, v)
                        instr = build(e)
                        if inc is not None:
                            instr.then_inc(inc[0].h, inc[1])

                sections[key](body)


def build_nc():
    nc = bass.Bass("TRN2", target_bir_lowering=False)
    p = Prog(nc)

    dx = nc.dram_tensor("x", [T, D], F32, kind="ExternalInput").ap()
    dy = nc.dram_tensor("y", [T, D], F32, kind="ExternalOutput").ap()
    dwg = nc.dram_tensor("w_gate", [INNER, D], F32, kind="ExternalInput").ap()
    dwu = nc.dram_tensor("w_up", [INNER, D], F32, kind="ExternalInput").ap()
    dwd = nc.dram_tensor("w_down", [D, INNER], F32, kind="ExternalInput").ap()
    d_idb = nc.dram_tensor("idn_bf16", [128, 128], BF16, kind="ExternalInput").ap()
    dx_cur = nc.dram_tensor("x_cur", [T, D], F32).ap()
    dwgT = nc.dram_tensor("wgT", [8 * 8 * 128, 512], BF16).ap()
    dwuT = nc.dram_tensor("wuT", [8 * 8 * 128, 512], BF16).ap()
    dwdT = nc.dram_tensor("wdT", [2 * 32 * 128, 512], BF16).ap()
    dq2T = [nc.dram_tensor(f"q2T{i}", [NTB * 32 * 128, TB], BF16).ap()
            for i in range(2)]
    if STAGE >= 2:
        dwq = nc.dram_tensor("Wq", [D, D], F32, kind="ExternalInput").ap()
        dwl = nc.dram_tensor("Wl", [D, LAT], F32, kind="ExternalInput").ap()
        dwk = nc.dram_tensor("Wk", [LAT, D], F32, kind="ExternalInput").ap()
        dwv = nc.dram_tensor("Wv", [LAT, D], F32, kind="ExternalInput").ap()
        dwo = nc.dram_tensor("Wo", [D, D], F32, kind="ExternalInput").ap()
        dbq = nc.dram_tensor("bq", [D], F32, kind="ExternalInput").ap()
        dbl = nc.dram_tensor("bl", [LAT], F32, kind="ExternalInput").ap()
        dbk = nc.dram_tensor("bk", [D], F32, kind="ExternalInput").ap()
        dbv = nc.dram_tensor("bv", [D], F32, kind="ExternalInput").ap()
        dbo = nc.dram_tensor("bo", [D], F32, kind="ExternalInput").ap()
        d_idf = nc.dram_tensor("idn_f32", [128, 128], F32, kind="ExternalInput").ap()
        lat_cc_in = nc.dram_tensor("lat_cc_in", [4 * 128, T], BF16)
        lat_cc_out = nc.dram_tensor("lat_cc_out", [8 * 128, T], BF16)
        doT_hi = nc.dram_tensor("oT_hi", [D, T], BF16).ap()
        doT_lo = nc.dram_tensor("oT_lo", [D, T], BF16).ap()
        dx_cur2 = nc.dram_tensor("x_cur2", [T, D], F32).ap()

    s_misc = p.sem("s_misc")
    s_x = [p.sem(f"s_x{i}") for i in range(2)]
    s_w = [p.sem(f"s_w{i}") for i in range(2)]
    s_wg = [p.sem(f"s_wg{i}") for i in range(2)]
    s_wu = [p.sem(f"s_wu{i}") for i in range(2)]
    s_wd = p.sem("s_wd")
    s_lx = [p.sem(f"s_lx{i}") for i in range(2)]
    s_sto = [p.sem(f"s_sto{i}") for i in range(2)]
    s_pst = [p.sem(f"s_pst{i}") for i in range(2)]
    s_q2s = p.sem("s_q2s")
    s_q2l = p.sem("s_q2l")
    s_pe = p.sem("s_pe")
    s_ve = p.sem("s_ve")
    s_se = p.sem("s_se")
    s_gc = p.sem("s_gc")
    s_x2 = [p.sem(f"s_x2{i}") for i in range(2)]
    s_wp = p.sem("s_wp")
    s_cin = p.sem("s_cin")
    s_cc = p.sem("s_cc")
    s_lat = p.sem("s_lat")
    s_ot = [p.sem(f"s_ot{i}") for i in range(2)]
    s_o3 = [p.sem(f"s_o3{i}") for i in range(2)]
    s_rb = p.sem("s_rb")
    s_wo = p.sem("s_wo")
    p.auto = {"v": s_ve, "s": s_se, "g": s_gc}

    def sb(name, shape, dtype, off):
        assert off + int(np.prod(shape[1:])) * mybir.dt.size(dtype) <= 224 * KB, name
        return nc.alloc_sbuf_tensor_at(name, shape, dtype, offset=off)

    _cur = [24 * KB]

    def sbp(name, shape, dtype):
        off = (_cur[0] + 31) // 32 * 32
        _cur[0] = off + int(np.prod(shape[1:])) * mybir.dt.size(dtype)
        assert _cur[0] <= 28 * KB
        return sb(name, shape, dtype, off)

    c_misc = sbp("c_misc", [128, 16], F32)
    invact1 = sbp("invact1", [128, NT], F32)
    scg_all = sbp("scg_all", [128, NT], F32)
    scu_all = sbp("scu_all", [128, NT], F32)
    scd_all = sbp("scd_all", [128, NT], F32)
    ws_g = sbp("ws_g", [128, 1], F32)
    ws_u = sbp("ws_u", [128, 1], F32)
    ws_d = sbp("ws_d", [128, 1], F32)
    iws_g = sbp("iws_g", [128, 1], F32)
    iws_u = sbp("iws_u", [128, 1], F32)
    iws_d = sbp("iws_d", [128, 1], F32)
    ones_row = sbp("ones_row", [1, 128], F32)
    ones_col = sbp("ones_col", [128, 1], F32)
    accp = sbp("accp", [128, 72], F32)
    idn_b = sbp("idn_b", [128, 128], BF16)
    ln_mean = sbp("ln_mean", [128, NT], F32)
    ln_rstd = sbp("ln_rstd", [128, NT], F32)
    q_sc = sbp("q_sc", [128, 8], F32)
    st6 = sbp("st6", [128, 16], F32)
    mv2 = sbp("mv2", [128, 2], F32)
    qsc2 = sbp("qsc2", [128, 8], F32)
    c_eps = sbp("c_eps", [128, 1], F32)
    xq1T = sb("xq1T", [128, 8, T], BF16, 28 * KB)

    A = 60 * KB
    gu = [sb(f"gu{t}", [128, INNER], F32, A + t * 16 * KB) for t in range(4)]
    q2T = sb("q2T", [128, 32, TB], BF16, A + 64 * KB)
    q2tm = sb("q2tm", [128, INNER], BF16, A + 96 * KB)
    wsl = [sb(f"wsl{i}", [128, 8, 512], BF16, A + 104 * KB + i * 8 * KB)
           for i in range(4)]
    wdl = sb("wdl", [128, 32, 512], BF16, A + 104 * KB)
    sg = [sb(f"sg{i}", [128, 512], F32, A + 136 * KB + i * 2 * KB) for i in range(2)]
    su = [sb(f"su{i}", [128, 512], F32, A + 140 * KB + i * 2 * KB) for i in range(2)]
    dn = [sb(f"dn{i}", [128, 512], F32, A + 144 * KB + i * 2 * KB) for i in range(2)]
    xio = [sb(f"xio{i}", [128, 512], F32, A + 148 * KB + i * 2 * KB) for i in range(2)]
    qs_i = sb("qs_i", [128, 512], I32, A + 152 * KB)
    qs_b = [sb(f"qs_b{i}", [128, 512], F32, A + 154 * KB + i * 2 * KB)
            for i in range(2)]
    qs_e = [sb(f"qs_e{i}", [128, 512], F32, A + 158 * KB + i * 2 * KB)
            for i in range(2)]
    qs_c = [sb(f"qs_c{i}", [128, 512], F32, A + 144 * KB + i * 2 * KB)
            for i in range(2)]

    wbuf = [sb(f"wbuf{i}", [128, INNER], F32, A + i * 16 * KB) for i in range(2)]
    wscr = sb("wscr", [128, INNER], F32, A + 32 * KB)
    wgl = sb("wgl", [128, INNER], F32, A + 48 * KB)
    wqb = sb("wqb", [128, INNER], BF16, A + 64 * KB)
    asm = sb("asm", [128, 32, 512], BF16, A + 72 * KB)
    xc = [sb(f"xc{i}", [128, D], F32, A + 104 * KB + i * 4 * KB) for i in range(2)]
    g_qt_scr = (
        [sb("qt_f0", [128, D], F32, A + 112 * KB),
         sb("qt_f1", [128, D], F32, A + 136 * KB)],
        sb("qt_i", [128, D], I32, A + 116 * KB),
        sb("qt_e", [128, D], F32, A + 120 * KB),
        sb("qt_c", [128, D], F32, A + 124 * KB),
        sb("xq_tm", [128, D], BF16, A + 128 * KB),
        [sb("qt_b0", [128, D], F32, A + 132 * KB),
         sb("qt_b1", [128, D], F32, A + 140 * KB)])

    pb = [nc.alloc_psum_tensor(f"pb{i}", [128, 512], F32) for i in range(8)]
    pbb = [t.bitcast(BF16) for t in pb]
    bank_free = {i: [] for i in range(8)}

    c127 = c_misc[:, 0:1]
    c1 = c_misc[:, 1:2]

    def memset(ap, val):
        p.op("v", lambda e, ap=ap, val=val: e.memset(ap, val))

    memset(c_misc[:, 0:1], 127.0)
    memset(c_misc[:, 1:2], 1.0)
    memset(ones_row[:, :], 1.0)
    memset(ones_col[:, :], 1.0)
    memset(accp[:, :], 0.0)
    memset(c_eps[:, :], EPS)
    id_ld = p.op("y", lambda e: e.dma_start(out=idn_b[:, :], in_=d_idb),
                 inc=(s_misc, 16))

    x_src = dx.rearrange("(c p) d -> c p d", p=128)

    def ln_quant_p1(xbuf, invact_dst, ln_idx, first_waits=(), ow_waits=(),
                    scratch=None, qtb_free=None):
        qt_f, qt_i, _, _, _, qt_b = (scratch if scratch is not None
                                     else g_qt_scr)
        par = ln_idx % 2
        p.op("v", lambda e: e.bn_stats(st6[:, 0:6], xbuf[:, 0:512]),
             waits=list(first_waits))
        p.op("v", lambda e: e.bn_stats(st6[:, 6:12], xbuf[:, 512:1024]))
        v0 = p.op("v", lambda e: e.bn_aggr(mv2[:, :], st6[:, 0:12]), inc=(s_ve, 1))
        p.op("s", lambda e: e.activation(q_sc[:, 0:1], mv2[:, 1:2], AF.Ln,
                                         bias=c_eps[:, :]), waits=[(s_ve, v0)])
        s0 = p.op("s", lambda e: e.activation(ln_rstd[:, ln_idx:ln_idx + 1],
                                              q_sc[:, 0:1], AF.Exp, scale=-0.5),
                  inc=(s_se, 1))
        p.op("v", lambda e: e.tensor_copy(ln_mean[:, ln_idx:ln_idx + 1],
                                          mv2[:, 0:1]))
        p.op("v", lambda e: e.tensor_scalar(
            xbuf[:, :], xbuf[:, :], ln_mean[:, ln_idx:ln_idx + 1],
            ln_rstd[:, ln_idx:ln_idx + 1], ALU.subtract, ALU.mult),
            waits=[(s_se, s0)] + list(ow_waits))
        p.op("v", lambda e: e.reduce_max(q_sc[:, 1:2], xbuf[:, :], axis=AXL.X,
                                         apply_absolute_value=True))
        p.op("v", lambda e: e.tensor_single_scalar(q_sc[:, 2:3], q_sc[:, 1:2],
                                                   EPS, ALU.max))
        p.op("v", lambda e: e.reciprocal(q_sc[:, 6:7], q_sc[:, 2:3]))
        p.op("v", lambda e: e.tensor_single_scalar(q_sc[:, 3:4], q_sc[:, 6:7],
                                                   127.0, ALU.mult))
        p.op("v", lambda e: e.reciprocal(invact_dst, q_sc[:, 3:4]))
        vm = p.op("v", lambda e, par=par: e.tensor_single_scalar(
            qt_f[par][:, :], xbuf[:, :], q_sc[:, 3:4], ALU.mult),
            inc=(s_ve, 1))
        p.op("s", lambda e, par=par: e.activation(
            qt_i[:, :], qt_f[par][:, :], AF.Copy, bias=0.0, scale=1.0),
            waits=[(s_ve, vm)])
        sc2 = p.op("s", lambda e, par=par: e.activation(
            qt_b[par][:, :], qt_i[:, :], AF.Copy, bias=0.0, scale=1.0),
            waits=[(s_ve, qtb_free[par])], inc=(s_se, 1))
        return vm, sc2

    def ln_quant_p2(xqT_dst, tc, ln_idx, sc2, scratch=None, qtb_free=None):
        qt_f, _, qt_e, qt_c, xq_tm, qt_b = (scratch if scratch is not None
                                            else g_qt_scr)
        par = ln_idx % 2
        p.op("v", lambda e, par=par: e.tensor_tensor(
            qt_e[:, :], qt_f[par][:, :], qt_b[par][:, :], ALU.subtract),
            waits=[(s_se, sc2)])
        p.op("v", lambda e, par=par: e.scalar_tensor_tensor(
            qt_c[:, :], qt_e[:, :], 0.5, qt_b[par][:, :], ALU.is_gt, ALU.add))
        stt = p.op("v", lambda e: e.tensor_single_scalar(
            qt_e[:, :], qt_e[:, :], -0.5, ALU.is_lt), inc=(s_ve, 1))
        qtb_free[par] = stt
        vq = p.op("v", lambda e: e.tensor_tensor(xq_tm[:, :], qt_c[:, :],
                                                 qt_e[:, :], ALU.subtract),
                  inc=(s_ve, 1))
        last_ev = 0
        for half in range(2):
            m = None
            for j in range(4):
                ic = half * 4 + j
                w = ([(s_ve, vq), (s_misc, id_ld)] + bank_free[half]) if j == 0 else []
                m = p.op("t", lambda e, half=half, j=j, ic=ic: e.transpose(
                    pbb[half][:, j * 128:(j + 1) * 128],
                    xq_tm[:, ic * 128:(ic + 1) * 128], idn_b[:, :]),
                    waits=w, inc=((s_pe, 1) if j == 3 else None))
            ev = p.op("v", lambda e, half=half, tc=tc: e.tensor_copy(
                xqT_dst[:, half * 4:(half + 1) * 4, tc * 128:(tc + 1) * 128],
                pbb[half][:, 0:512]), waits=[(s_pe, m)], inc=(s_ve, 1))
            bank_free[half] = [(s_ve, ev)]
            last_ev = ev
        return vq, last_ev

    xbuf_free = {0: [], 1: []}
    qtb_free1 = {0: 0, 1: 0}
    xq_ev1 = []
    pend_ln = None
    for tc in range(NT):
        ld = p.op("y", lambda e, tc=tc: e.dma_start(
            out=xc[tc % 2][:, :], in_=x_src[tc]), waits=xbuf_free[tc % 2],
            inc=(s_x[tc % 2], 16))
        vm, sc2 = ln_quant_p1(xc[tc % 2], invact1[:, tc:tc + 1], tc,
                              first_waits=[(s_x[tc % 2], ld)], qtb_free=qtb_free1)
        if pend_ln is not None:
            _, ev = ln_quant_p2(xq1T, pend_ln[0], pend_ln[0], pend_ln[1],
                                qtb_free=qtb_free1)
            xq_ev1.append(ev)
        pend_ln = (tc, sc2)
        xbuf_free[tc % 2] = [(s_ve, vm)]
    _, ev = ln_quant_p2(xq1T, pend_ln[0], pend_ln[0], pend_ln[1],
                        qtb_free=qtb_free1)
    xq_ev1.append(ev)
    xq_done = ev

    mats = [("g", dwg, 32, D, dwgT, ws_g, iws_g, 0),
            ("u", dwu, 32, D, dwuT, ws_u, iws_u, 32),
            ("d", dwd, 8, INNER, dwdT, ws_d, iws_d, 64)]

    abs_done = {}
    smile = []
    for name, dw, n_chunks, C, _, _, _, col0 in mats:
        src = dw.rearrange("(c p) d -> c p d", p=128)
        for r in range(n_chunks):
            gi = len(smile)
            buf = wbuf[gi % 2]
            w = [(s_se, smile[gi - 2])] if gi >= 2 else []
            ld = p.op("y", lambda e, src=src, r=r, buf=buf, C=C: e.dma_start(
                out=buf[:, 0:C], in_=src[r]), waits=w, inc=(s_w[gi % 2], 16))
            sm = p.op("s", lambda e, buf=buf, C=C, col=col0 + r: e.activation(
                wscr[:, 0:C], buf[:, 0:C], AF.Abs,
                accum_out=accp[:, col:col + 1]),
                waits=[(s_w[gi % 2], ld)], inc=(s_se, 1))
            smile.append(sm)
        abs_done[name] = smile[-1]

    for name, dw, n_chunks, C, dwT, ws_dst, iws_dst, col0 in mats:
        v0 = p.op("v", lambda e, col0=col0, n=n_chunks: e.reduce_sum(
            q_sc[:, 4:5], accp[:, col0:col0 + n], axis=AXL.X),
            waits=[(s_se, abs_done[name])], inc=(s_ve, 1))
        m0 = p.op("t", lambda e: e.matmul(pb[6][0:1, 0:1], ones_col[:, :],
                                          q_sc[:, 4:5], start=True, stop=True),
                  waits=[(s_ve, v0)] + bank_free[6], inc=(s_pe, 1))
        nel = float(n_chunks * 128 * C)
        p.op("v", lambda e, nel=nel: e.tensor_single_scalar(
            q_sc[0:1, 5:6], pb[6][0:1, 0:1], 1.0 / nel, ALU.mult),
            waits=[(s_pe, m0)])
        v1 = p.op("v", lambda e: e.tensor_single_scalar(
            q_sc[0:1, 5:6], q_sc[0:1, 5:6], EPS, ALU.max), inc=(s_ve, 1))
        m1 = p.op("t", lambda e: e.matmul(pb[6][:, 1:2], ones_row[:, :],
                                          q_sc[0:1, 5:6], start=True, stop=True),
                  waits=[(s_ve, v1)], inc=(s_pe, 1))
        p.op("v", lambda e, iws=iws_dst: e.tensor_copy(iws[:, :], pb[6][:, 1:2]),
             waits=[(s_pe, m1)])
        vw = p.op("v", lambda e, ws=ws_dst, iws=iws_dst: e.reciprocal(ws[:, :], iws[:, :]),
                  inc=(s_ve, 1))
        bank_free[6] = [(s_ve, vw)]
    ws_ready = s_ve.n

    prep_vq = []
    prep_st = {}
    gg = 0
    for name, dw, n_chunks, C, dwT, ws_src, _, _ in mats:
        src = dw.rearrange("(c p) d -> c p d", p=128)
        n_ic = C // 128
        wide = (C == INNER)
        prep_st[name] = []
        for g in range(n_chunks // 4):
            for j in range(4):
                r = g * 4 + j
                gi = len(smile)
                buf = wbuf[gi % 2]
                par = 0 if wide else (r % 2)
                a0 = par * C
                w = [(s_se, smile[gi - 2])]
                if gi == 72:
                    w.append((s_ve, ws_ready))
                ld = p.op("y", lambda e, src=src, r=r, buf=buf, C=C: e.dma_start(
                    out=buf[:, 0:C], in_=src[r]), waits=w, inc=(s_w[gi % 2], 16))
                vq_w = prep_vq[-2:] if (wide or len(prep_st[name]) == 0 and
                                        g == 0 and j < 2) else prep_vq[-2:-1]
                sm = p.op("s", lambda e, buf=buf, C=C, ws=ws_src, a0=a0:
                          e.activation(
                              wscr[:, a0:a0 + C], buf[:, 0:C], AF.Copy,
                              scale=ws[:, :]),
                          waits=[(s_w[gi % 2], ld)] + vq_w, inc=(s_se, 1))
                smile.append(sm)
                p.op("v", lambda e, C=C, a0=a0: e.tensor_single_scalar(
                    wgl[:, a0:a0 + C], wscr[:, a0:a0 + C], 0.5, ALU.is_gt),
                    waits=[(s_se, sm)])
                p.op("v", lambda e, C=C, a0=a0: e.tensor_single_scalar(
                    wscr[:, a0:a0 + C], wscr[:, a0:a0 + C], -0.5, ALU.is_lt))
                vq = p.op("v", lambda e, C=C, a0=a0: e.tensor_tensor(
                    wqb[:, a0:a0 + C], wgl[:, a0:a0 + C], wscr[:, a0:a0 + C],
                    ALU.subtract),
                    inc=(s_ve, 1))
                prep_vq.append((s_ve, vq))
                for ic in range(n_ic):
                    bank = 2 + (ic % 2)
                    w2 = ([(s_ve, vq)] if ic < 2 else []) + bank_free[bank]
                    m = p.op("t", lambda e, bank=bank, ic=ic, a0=a0: e.transpose(
                        pbb[bank][:, 0:128],
                        wqb[:, a0 + ic * 128:a0 + (ic + 1) * 128],
                        idn_b[:, :]), waits=w2, inc=(s_pe, 1))
                    asm_w = [(s_pe, m)]
                    if j == 0 and ic == 0:
                        sts = prep_st[name]
                        if wide or len(sts) < 2:
                            asm_w += [(sp, sp.n) for sp in s_pst if sp.n > 0]
                        else:
                            asm_w.append(sts[-2])
                    slot = (0 if wide else (g % 2) * n_ic) + ic
                    ev = p.op("v", lambda e, slot=slot, j=j, bank=bank:
                              e.tensor_copy(
                                  asm[:, slot, j * 128:(j + 1) * 128],
                                  pbb[bank][:, 0:128]),
                              waits=asm_w, inc=(s_ve, 1))
                    bank_free[bank] = [(s_ve, ev)]
            asm_done = s_ve.n
            for ic in range(n_ic):
                slot = (0 if wide else (g % 2) * n_ic) + ic
                p.op("g", lambda e, dwT=dwT, g=g, ic=ic, n_ic=n_ic, slot=slot:
                     e.dma_start(
                         out=dwT[(g * n_ic + ic) * 128:
                                 (g * n_ic + ic + 1) * 128, :],
                         in_=asm[:, slot, :]),
                     waits=[(s_ve, asm_done)] if ic == 0 else [],
                     inc=(s_pst[gg % 2], 16))
            prep_st[name].append((s_pst[gg % 2], s_pst[gg % 2].n))
            gg += 1
    prep_done_gd = [(sp, sp.n) for sp in s_pst]
    prep_s_last = smile[-1]
    prep_v_last = prep_vq[-1]

    def emit_ffn(xqT, x_in_src, x_out_dst, extra_out_dst, scg, scu, scd,
                 first_waits, xq_ev, dq2, sc_gate=(), gu_gate=(), ev_gate=(),
                 wst=None):
        xin = x_in_src.rearrange("(c p) d -> c p d", p=128)
        xout = x_out_dst.rearrange("(c p) d -> c p d", p=128)
        xtra = (extra_out_dst.rearrange("(c p) d -> c p d", p=128)
                if extra_out_dst is not None else None)
        wg_src = dwgT.rearrange("(g a p) n -> g p a n", g=8, p=128)
        wu_src = dwuT.rearrange("(g a p) n -> g p a n", g=8, p=128)
        wd_src = dwdT.rearrange("(g a p) n -> g p a n", g=2, p=128)
        q2_st = dq2.rearrange("(b a p) n -> b p a n", b=NTB, p=128)

        mm_after = {}
        sgsu_free = {}
        q2s_done = {}
        qsb_free = {0: 0, 1: 0}
        qec_free = {0: 0, 1: 0}
        first = True
        first_s1 = list(sc_gate)
        first_vg = list(gu_gate)
        first_ev = list(ev_gate)
        for tb in range(NTB):
            for ot in range(8):
                gi = tb * 8 + ot
                gb, ub = wsl[ot % 2], wsl[2 + (ot % 2)]
                if first:
                    w = list(first_waits) + [(s_misc, id_ld)]
                    first = False
                elif gi >= 2:
                    w = [(s_pe, mm_after[gi - 2])]
                else:
                    w = []
                if wst is not None and tb == 0:
                    w = w + [wst["g"][ot]]
                lg = p.op("y", lambda e, gb=gb, ot=ot: e.dma_start(
                    out=gb[:, :, :], in_=wg_src[ot]), waits=w,
                    inc=(s_wg[ot % 2], 16))
                lu = p.op("y", lambda e, ub=ub, ot=ot: e.dma_start(
                    out=ub[:, :, :], in_=wu_src[ot]),
                    waits=([wst["u"][ot]]
                           if wst is not None and tb == 0 else []),
                    inc=(s_wu[ot % 2], 16))
                for tt in range(4):
                    tc = tb * 4 + tt
                    for bank, wslice, lsem in ((tt % 2, gb, (s_wg[ot % 2], lg)),
                                               (2 + (tt % 2), ub, (s_wu[ot % 2], lu))):
                        for ic in range(8):
                            wts = ([lsem, (s_ve, xq_ev[tc])] +
                                   bank_free[bank]) if ic == 0 else []
                            p.op("t", lambda e, bank=bank, wslice=wslice, ic=ic,
                                 tc=tc: e.matmul(
                                     pb[bank][:, :],
                                     xqT[:, ic, tc * 128:(tc + 1) * 128],
                                     wslice[:, ic, :],
                                     start=(ic == 0), stop=(ic == 7)),
                                 waits=wts, inc=((s_pe, 1) if ic == 7 else None))
                    mm_done = s_pe.n
                    par = tt % 2
                    s1 = p.op("s", lambda e, b=par, par=par, tc=tc:
                              e.activation(sg[par][:, :], pb[b][:, :], AF.Silu,
                                           scale=scg[:, tc:tc + 1]),
                              waits=[(s_pe, mm_done)] + sgsu_free.get(par, []) +
                              first_s1, inc=(s_se, 1))
                    first_s1 = []
                    v1 = p.op("s", lambda e, b=2 + par, par=par, tc=tc:
                              e.activation(su[par][:, :], pb[b][:, :], AF.Copy,
                                           scale=scu[:, tc:tc + 1]),
                              waits=[(s_pe, mm_done)], inc=(s_se, 1))
                    bank_free[par] = [(s_se, s1)]
                    bank_free[2 + par] = [(s_se, v1)]
                    vg = p.op("g", lambda e, par=par, tt=tt, ot=ot: e.tensor_tensor(
                        gu[tt][:, ot * 512:(ot + 1) * 512], sg[par][:, :],
                        su[par][:, :], ALU.mult),
                        waits=[(s_se, v1)] + first_vg, inc=(s_gc, 1))
                    first_vg = []
                    sgsu_free[par] = [(s_gc, vg)]
                mm_after[gi] = s_pe.n
            for tt in range(4):
                tc = tb * 4 + tt
                p.op("v", lambda e, tt=tt: e.reduce_max(
                    qsc2[:, 0:1], gu[tt][:, :], axis=AXL.X,
                    apply_absolute_value=True),
                    waits=[(s_gc, s_gc.n)])
                p.op("v", lambda e: e.tensor_single_scalar(
                    qsc2[:, 1:2], qsc2[:, 0:1], EPS, ALU.max))
                p.op("v", lambda e: e.reciprocal(qsc2[:, 6:7], qsc2[:, 1:2]))
                p.op("v", lambda e: e.tensor_single_scalar(
                    qsc2[:, 2:3], qsc2[:, 6:7], 127.0, ALU.mult))
                p.op("v", lambda e: e.reciprocal(qsc2[:, 3:4], qsc2[:, 2:3]))
                p.op("v", lambda e: e.tensor_tensor(
                    qsc2[:, 4:5], qsc2[:, 3:4], iws_d[:, :], ALU.mult))
                p.op("v", lambda e, tc=tc: e.tensor_single_scalar(
                    scd[:, tc:tc + 1], qsc2[:, 4:5], 0.5, ALU.mult))
                vq_sl = []
                for sl in range(8):
                    a, b2 = sl * 512, (sl + 1) * 512
                    par2 = sl % 2
                    v1 = p.op("v", lambda e, tt=tt, a=a, b2=b2:
                              e.tensor_single_scalar(
                                  gu[tt][:, a:b2], gu[tt][:, a:b2],
                                  qsc2[:, 2:3], ALU.mult), inc=(s_ve, 1))
                    p.op("s", lambda e, tt=tt, a=a, b2=b2: e.activation(
                        qs_i[:, :], gu[tt][:, a:b2], AF.Copy,
                        bias=0.0, scale=1.0), waits=[(s_ve, v1)])
                    s3 = p.op("s", lambda e, par2=par2: e.activation(
                        qs_b[par2][:, :], qs_i[:, :], AF.Copy,
                        bias=0.0, scale=1.0),
                        waits=[(s_ve, qsb_free[par2])], inc=(s_se, 1))
                    p.op("v", lambda e, tt=tt, a=a, b2=b2, par2=par2:
                         e.tensor_tensor(
                             qs_e[par2][:, :], gu[tt][:, a:b2],
                             qs_b[par2][:, :], ALU.subtract),
                         waits=[(s_se, s3), (s_gc, qec_free[par2])])
                    p.op("v", lambda e, par2=par2: e.scalar_tensor_tensor(
                        qs_c[par2][:, :], qs_e[par2][:, :], 0.5,
                        qs_b[par2][:, :], ALU.is_gt, ALU.add))
                    v6 = p.op("v", lambda e, par2=par2: e.tensor_single_scalar(
                        qs_e[par2][:, :], qs_e[par2][:, :], -0.5,
                        ALU.is_lt), inc=(s_ve, 1))
                    qsb_free[par2] = v6
                    vq = p.op("g", lambda e, a=a, b2=b2, par2=par2:
                              e.tensor_tensor(
                                  q2tm[:, a:b2], qs_c[par2][:, :],
                                  qs_e[par2][:, :], ALU.subtract),
                              waits=[(s_ve, v6)], inc=(s_gc, 1))
                    qec_free[par2] = vq
                    vq_sl.append([(s_gc, vq)])
                for half in range(8):
                    bank = 4 + (half % 2)
                    m = None
                    for j in range(4):
                        ic = half * 4 + j
                        w3 = (vq_sl[half] + [(s_misc, id_ld), (s_q2s, s_q2s.n)] +
                              bank_free[bank]) if j == 0 else []
                        m = p.op("t", lambda e, bank=bank, j=j, ic=ic:
                                 e.transpose(pbb[bank][:, j * 128:(j + 1) * 128],
                                             q2tm[:, ic * 128:(ic + 1) * 128],
                                             idn_b[:, :]),
                                 waits=w3, inc=((s_pe, 1) if j == 3 else None))
                    ev = p.op("v", lambda e, bank=bank, half=half, tt=tt:
                              e.tensor_copy(
                                  q2T[:, half * 4:(half + 1) * 4,
                                      tt * 128:(tt + 1) * 128],
                                  pbb[bank][:, 0:512]),
                              waits=[(s_pe, m)] + first_ev, inc=(s_ve, 1))
                    first_ev = []
                    bank_free[bank] = [(s_ve, ev)]
            st = p.op("g", lambda e, tb=tb: e.dma_start(
                out=q2_st[tb], in_=q2T[:, :, :]),
                waits=[(s_ve, s_ve.n)], inc=(s_q2s, 16))
            q2s_done[tb] = st
        xio_free = {}
        q2l_done = {}
        for ot2 in range(2):
            ldd = p.op("y", lambda e, ot2=ot2: e.dma_start(
                out=wdl[:, :, :], in_=wd_src[ot2]),
                waits=([(s_pe, s_pe.n), (s_q2s, q2s_done[0])] if ot2 == 0
                       else [(s_pe, s_pe.n)]) +
                ([wst["d"][ot2]] if wst is not None else []),
                inc=(s_wd, 16))
            for tb in range(NTB):
                lq = p.op("y", lambda e, tb=tb: e.dma_start(
                    out=q2T[:, :, :], in_=q2_st[tb]),
                    waits=[(s_q2s, q2s_done[NTB - 1]), (s_pe, s_pe.n)],
                    inc=(s_q2l, 16))
                for tt in range(4):
                    tc = tb * 4 + tt
                    bank = 6 + (tt % 2)
                    for ic in range(32):
                        wts = ([(s_wd, ldd), (s_q2l, lq)] + bank_free[bank]) \
                            if ic == 0 else []
                        p.op("t", lambda e, bank=bank, ic=ic, tt=tt: e.matmul(
                            pb[bank][:, :], q2T[:, ic, tt * 128:(tt + 1) * 128],
                            wdl[:, ic, :], start=(ic == 0), stop=(ic == 31)),
                            waits=wts, inc=((s_pe, 1) if ic == 31 else None))
                    mmd = s_pe.n
                    par = tt % 2
                    lx = p.op("y", lambda e, tc=tc, ot2=ot2, par=par: e.dma_start(
                        out=xio[par][:, :], in_=xin[tc][:, ot2 * 512:(ot2 + 1) * 512]),
                        waits=xio_free.get(par, []), inc=(s_lx[par], 16))
                    s1 = p.op("s", lambda e, bank=bank, par=par, tc=tc:
                              e.activation(dn[par][:, :], pb[bank][:, :],
                                           AF.Copy, scale=scd[:, tc:tc + 1]),
                              waits=[(s_pe, mmd), (s_ve, s_ve.n)], inc=(s_se, 1))
                    bank_free[bank] = [(s_se, s1)]
                    vr = p.op("v", lambda e, par=par: e.tensor_tensor(
                        xio[par][:, :], xio[par][:, :], dn[par][:, :], ALU.add),
                        waits=[(s_se, s1), (s_lx[par], lx)], inc=(s_ve, 1))
                    sd = p.op("g", lambda e, tc=tc, ot2=ot2, par=par: e.dma_start(
                        out=xout[tc][:, ot2 * 512:(ot2 + 1) * 512],
                        in_=xio[par][:, :]),
                        waits=[(s_ve, vr)], inc=(s_sto[par], 16))
                    if xtra is not None:
                        sd = p.op("g", lambda e, tc=tc, ot2=ot2, par=par:
                                  e.dma_start(
                                      out=xtra[tc][:, ot2 * 512:(ot2 + 1) * 512],
                                      in_=xio[par][:, :]), inc=(s_sto[par], 16))
                    xio_free[par] = [(s_sto[par], sd)]
        return (s_sto[0].n, s_sto[1].n)

    def emit_attention(x_src_dram, start_waits):
        PB = 28 * KB
        _c = [PB]

        def sba(name, shape, dtype):
            off = (_c[0] + 31) // 32 * 32
            _c[0] = off + int(np.prod(shape[1:])) * mybir.dt.size(dtype)
            assert _c[0] <= 224 * KB, (name, _c[0])
            return sb(name, shape, dtype, off)

        xn2T_hi = sba("xn2T_hi", [128, 8, T], BF16)
        xn2T_lo = sba("xn2T_lo", [128, 8, T], BF16)
        latT_hi = sba("latT_hi", [128, 2, S], BF16)
        latT_lo = sba("latT_lo", [128, 2, S], BF16)
        KTs = [sba(f"KTs{i}", [128, S], BF16) for i in range(2)]
        V65 = [sba(f"V65_{i}", [128, 32, 65], BF16) for i in range(2)]
        Vlo = [sba(f"Vlo_{i}", [128, 32, 64], BF16) for i in range(2)]
        _qhid0_off = [(_c[0] + 31) // 32 * 32]
        QhiD = [sba(f"QhiD{i}", [128, T], BF16) for i in range(2)]
        _qlod_off = [(_c[0] + 31) // 32 * 32]
        QloD = [sba(f"QloD{i}", [128, T], BF16) for i in range(2)]
        pT_f = [sba(f"pT_f{i}", [128, 512], F32) for i in range(4)]
        pT_hi = [sba(f"pT_hi{i}", [128, 512], BF16) for i in range(6)]
        pT_lo = [sba(f"pT_lo{i}", [128, 512], BF16) for i in range(6)]
        rb_sb = sba("rb_sb", [64, 512], F32)
        o_scr = sba("o_scr", [64, 512], F32)
        oT_hi_s = [sba(f"oT_hi_s{i}", [64, 512], BF16) for i in range(2)]
        oT_lo_s = [sba(f"oT_lo_s{i}", [64, 512], BF16) for i in range(2)]
        wq_f = sba("wq_f", [128, 8, 128], F32)
        wk_f = sba("wk_f", [128, 2, 128], F32)
        wv_f = sba("wv_f", [128, 2, 128], F32)
        wv_hi = sba("wv_hi", [128, 2, 128], BF16)
        wv_lo = sba("wv_lo", [128, 2, 128], BF16)
        wl_hi = sb("wl_hi", [128, 8, 256], BF16, _qlod_off[0])
        wl_lo = sb("wl_lo", [128, 8, 256], BF16, _qlod_off[0] + 4 * KB)
        bqd = sba("bqd", [128, 16], F32)
        bkd = sba("bkd", [128, 16], F32)
        bl_sb = sba("bl_sb", [128, 2], F32)
        bv_row = sba("bv_row", [1, 128], F32)
        bv_bc = sba("bv_bc", [128, 128], F32)
        _wqd0_off = [(_c[0] + 31) // 32 * 32]
        wqd_hi = [sba(f"wqd_hi{i}", [128, 8, 128], BF16) for i in range(2)]
        wqd_lo = [sba(f"wqd_lo{i}", [128, 8, 128], BF16) for i in range(2)]
        wkd_hi = [sba(f"wkd_hi{i}", [128, 2, 128], BF16) for i in range(2)]
        wkd_lo = [sba(f"wkd_lo{i}", [128, 2, 128], BF16) for i in range(2)]
        scr4k = sba("scr4k", [128, D], F32)
        xc2 = [scr4k, sb("xc2b", [128, D], F32, _wqd0_off[0])]

        class _ColView:
            def __init__(self, t, c0, w):
                self.t, self.c0, self.w = t, c0, w

            def __getitem__(self, idx):
                ps, cs = idx
                a = self.c0 + (cs.start or 0)
                b = self.c0 + (cs.stop if cs.stop is not None else self.w)
                return self.t[ps, a:b]

        et0 = _ColView(scr4k, 0, 512)
        sqs = _ColView(scr4k, 512, 512)
        mqk = sba("mqk", [128, 8], F32)
        negb = [sba(f"negb{i}", [128, 1], F32) for i in range(2)]
        ones64 = sba("ones64", [128, 64], F32)
        idn_f = sba("idn_f", [128, 128], F32)
        wl_f = sb("wl_f", [128, 8, 256], F32, _qhid0_off[0])

        ldw = p.op("y", lambda e: e.dma_start(out=idn_f[:, :], in_=d_idf),
                   waits=list(start_waits), inc=(s_wo, 16))
        ldw = p.op("y", lambda e: e.dma_start(
            out=wl_f[:, :, :], in_=dwl.rearrange("(a p) l -> p a l", p=128)),
            inc=(s_wo, 16))
        for dst, dsrc in ((bqd, dbq), (bkd, dbk)):
            for lohalf in range(2):
                ldw = p.op("y", lambda e, dst=dst, dsrc=dsrc, lohalf=lohalf:
                           e.dma_start(
                               out=dst[lohalf * 64:(lohalf + 1) * 64, :],
                               in_=dsrc.rearrange("(a p) -> p a", p=64),
                               allow_slow_non_contiguous=True),
                           inc=(s_wo, 16))
        ldw = p.op("y", lambda e: e.dma_start(
            out=bl_sb[:, :], in_=dbl.rearrange("(a p) -> p a", p=128),
            allow_slow_non_contiguous=True),
            inc=(s_wo, 16))
        w_onoff = s_wo.n
        p.op("v", lambda e: e.memset(V65[0][:, :, 64:65], 1.0),
             waits=list(start_waits))
        memset(ones64[:, :], 1.0)
        memset(V65[1][:, :, 64:65], 1.0)
        p.op("v", lambda e: e.tensor_copy(wl_hi[:, :, :], wl_f[:, :, :]),
             waits=[(s_wo, w_onoff)])
        vwl = p.op("v", lambda e: e.scalar_tensor_tensor(
            wl_lo[:, :, :], wl_hi[:, :, :], -1.0, wl_f[:, :, :],
            ALU.mult, ALU.add))

        xin = x_src_dram.rearrange("(c p) d -> c p d", p=128)
        xfree = {0: list(start_waits), 1: list(start_waits)}
        for tc in range(NT):
            ld = p.op("y", lambda e, tc=tc: e.dma_start(
                out=xc2[tc % 2][:, :], in_=xin[tc]),
                waits=xfree[tc % 2], inc=(s_x2[tc % 2], 16))
            xb = xc2[tc % 2]
            p.op("v", lambda e, xb=xb: e.bn_stats(st6[:, 0:6], xb[:, 0:512]),
                 waits=[(s_x2[tc % 2], ld)])
            p.op("v", lambda e, xb=xb: e.bn_stats(st6[:, 6:12], xb[:, 512:1024]))
            v0 = p.op("v", lambda e: e.bn_aggr(mv2[:, :], st6[:, 0:12]))
            p.op("s", lambda e: e.activation(q_sc[:, 0:1], mv2[:, 1:2], AF.Ln,
                                             bias=c_eps[:, :]), waits=[(s_ve, v0)])
            s0 = p.op("s", lambda e: e.activation(q_sc[:, 1:2], q_sc[:, 0:1],
                                                  AF.Exp, scale=-0.5))
            vn = p.op("v", lambda e, xb=xb: e.tensor_scalar(
                xb[:, :], xb[:, :], mv2[:, 0:1], q_sc[:, 1:2],
                ALU.subtract, ALU.mult), waits=[(s_se, s0)])
            for half in range(2):
                m = None
                for j in range(4):
                    ic = half * 4 + j
                    w = ([(s_ve, vn), (s_wo, w_onoff)] + bank_free[half]) \
                        if j == 0 else []
                    m = p.op("t", lambda e, half=half, j=j, ic=ic, xb=xb:
                             e.transpose(pb[half][:, j * 128:(j + 1) * 128],
                                         xb[:, ic * 128:(ic + 1) * 128],
                                         idn_f[:, :]),
                             waits=w, inc=((s_pe, 1) if j == 3 else None))
                p.op("v", lambda e, half=half, tc=tc: e.tensor_copy(
                    xn2T_hi[:, half * 4:(half + 1) * 4, tc * 128:(tc + 1) * 128],
                    pb[half][:, :]), waits=[(s_pe, m)])
                ev = p.op("v", lambda e, half=half, tc=tc: e.scalar_tensor_tensor(
                    xn2T_lo[:, half * 4:(half + 1) * 4, tc * 128:(tc + 1) * 128],
                    xn2T_hi[:, half * 4:(half + 1) * 4, tc * 128:(tc + 1) * 128],
                    -1.0, pb[half][:, :], ALU.mult, ALU.add))
                bank_free[half] = [(s_ve, ev)]
            xfree[tc % 2] = [(s_ve, vn), (s_pe, s_pe.n)]
        xn2_done = s_ve.n

        terms = [(wl_hi, xn2T_hi), (wl_hi, xn2T_lo), (wl_lo, xn2T_hi)]
        for lc in range(2):
            for t4 in range(4):
                bank = 4 + (t4 % 2)
                n_mm = 0
                for whi, xn in terms:
                    for ic in range(8):
                        w = ([(s_ve, max(xn2_done, vwl))] + bank_free[bank]) \
                            if n_mm == 0 else []
                        p.op("t", lambda e, whi=whi, xn=xn, ic=ic, lc=lc, t4=t4,
                             bank=bank, st=(n_mm == 0), sp=(n_mm == 23): e.matmul(
                                 pb[bank][:, :], whi[:, ic, lc * 128:(lc + 1) * 128],
                                 xn[:, ic, t4 * 512:(t4 + 1) * 512],
                                 start=st, stop=sp),
                             waits=w, inc=((s_pe, 1) if n_mm == 23 else None))
                        n_mm += 1
                mml = s_pe.n
                s1 = p.op("s", lambda e, bank=bank, lc=lc: e.activation(
                    et0[:, :], pb[bank][:, :], AF.Identity, bias=bl_sb[:, lc:lc + 1]),
                    waits=[(s_pe, mml), (s_ve, s_ve.n)], inc=(s_se, 1))
                bank_free[bank] = [(s_se, s1)]
                p.op("v", lambda e, lc=lc, t4=t4: e.tensor_copy(
                    latT_hi[:, lc, t4 * 512:(t4 + 1) * 512], et0[:, :]),
                    waits=[(s_se, s1)])
                p.op("v", lambda e, lc=lc, t4=t4: e.scalar_tensor_tensor(
                    latT_lo[:, lc, t4 * 512:(t4 + 1) * 512],
                    latT_hi[:, lc, t4 * 512:(t4 + 1) * 512], -1.0, et0[:, :],
                    ALU.mult, ALU.add))
        lat_own = s_ve.n

        ccin = lat_cc_in.ap()
        for i_sec, (tsr, lc) in enumerate([(latT_hi, 0), (latT_hi, 1),
                                           (latT_lo, 0), (latT_lo, 1)]):
            p.op("y", lambda e, tsr=tsr, lc=lc, i_sec=i_sec: e.dma_start(
                out=ccin[i_sec * 128:(i_sec + 1) * 128, :],
                in_=tsr[:, lc, 0:T]),
                waits=[(s_ve, lat_own)] if i_sec == 0 else [], inc=(s_cin, 16))
        cin_done = s_cin.n
        cc = p.op("g", lambda e: e.collective_compute(
            "AllGather", ALU.bypass, replica_groups=[[0, 1], [2, 3], [4, 5], [6, 7]],
            ins=[lat_cc_in.ap().opt()], outs=[lat_cc_out.ap().opt()]),
            waits=[(s_cin, cin_done)], inc=(s_cc, 1))
        ccout = lat_cc_out.ap()
        for r in range(2):
            for i_sec, (tsr, lc) in enumerate([(latT_hi, 0), (latT_hi, 1),
                                               (latT_lo, 0), (latT_lo, 1)]):
                p.op("y", lambda e, r=r, tsr=tsr, lc=lc, i_sec=i_sec: e.dma_start(
                    out=tsr[:, lc, r * T:(r + 1) * T],
                    in_=ccout[r * 512 + i_sec * 128:r * 512 + (i_sec + 1) * 128, :]),
                    waits=[(s_cc, cc)] if (r == 0 and i_sec == 0) else [],
                    inc=(s_lat, 16))
        lat_full = s_lat.n

        wq_src = dwq.rearrange("(a p) o -> p a o", p=128)
        wk_src = dwk.rearrange("(a p) o -> p a o", p=128)
        wv_src = dwv.rearrange("(a p) o -> p a o", p=128)
        pair_w_done = {}
        split_done = {}
        pend_den = []
        sqs_free = []

        def flush_denominator():
            ob, hg, qt, mo = pend_den.pop(0)
            p.op("s", lambda e, ob=ob: e.activation(
                sqs[64:65, 0:512], pb[ob][64:65, :], AF.Ln),
                waits=[(s_pe, mo)] + list(sqs_free))
            vr1 = p.op("s", lambda e: e.activation(
                sqs[64:65, 0:512], sqs[64:65, 0:512], AF.Exp, scale=-1.0),
                inc=(s_se, 1))
            m_rb = p.op("t", lambda e: e.matmul(
                pb[6][0:64, 0:512], ones64[64:65, 0:64], sqs[64:65, 0:512],
                start=True, stop=True),
                waits=[(s_se, vr1)] + bank_free[6], inc=(s_pe, 1))
            sqs_free[:] = [(s_pe, m_rb)]
            p.op("v", lambda e: e.tensor_copy(rb_sb[:, :], pb[6][0:64, 0:512]),
                 waits=[(s_pe, m_rb)])
            bank_free[6] = [(s_ve, s_ve.n)]
            p.op("v", lambda e, ob=ob: e.tensor_tensor(
                o_scr[:, :], pb[ob][0:64, :], rb_sb[:, :], ALU.mult))
            spar = (hg * 4 + qt) % 2
            p.op("v", lambda e, spar=spar: e.tensor_copy(
                oT_hi_s[spar][:, :], o_scr[:, :]),
                waits=list(oT_free.get(spar, [])))
            vos = p.op("v", lambda e, spar=spar: e.scalar_tensor_tensor(
                oT_lo_s[spar][:, :], oT_hi_s[spar][:, :], -1.0,
                o_scr[:, :], ALU.mult, ALU.add))
            bank_free[ob] = [(s_ve, vos)]
            so = p.op("g", lambda e, hg=hg, qt=qt, spar=spar: e.dma_start(
                out=doT_hi[hg * 64:(hg + 1) * 64, qt * 512:(qt + 1) * 512],
                in_=oT_hi_s[spar][:, :]),
                waits=[(s_ve, vos)], inc=(s_ot[spar], 16))
            so = p.op("g", lambda e, hg=hg, qt=qt, spar=spar: e.dma_start(
                out=doT_lo[hg * 64:(hg + 1) * 64, qt * 512:(qt + 1) * 512],
                in_=oT_lo_s[spar][:, :]), inc=(s_ot[spar], 16))
            oT_free[spar] = [(s_ot[spar], so)]

        def load_pair_slices(pr):
            w = ([(s_ve, split_done[pr - 1]), (s_pe, s_pe.n)]
                 if pr >= 1 else [])
            p.op("y", lambda e, pr=pr: e.dma_start(
                out=wq_f[:, :, :], in_=wq_src[:, :, pr * 128:(pr + 1) * 128]),
                waits=w, inc=(s_wp, 16))
            p.op("y", lambda e, pr=pr: e.dma_start(
                out=wk_f[:, :, :], in_=wk_src[:, :, pr * 128:(pr + 1) * 128]),
                inc=(s_wp, 16))
            p.op("y", lambda e, pr=pr: e.dma_start(
                out=wv_f[:, :, :], in_=wv_src[:, :, pr * 128:(pr + 1) * 128]),
                inc=(s_wp, 16))
            p.op("y", lambda e, pr=pr: e.dma_start(
                out=bv_row[0:1, :], in_=dbv[None, pr * 128:(pr + 1) * 128]),
                inc=(s_wp, 16))
            pair_w_done[pr] = s_wp.n

        load_pair_slices(0)

        for pr in range(8):
            wsp = [(s_wp, pair_w_done[pr]), (s_pe, s_pe.n)]
            first = True
            for hh in range(2):
                for half in range(2):
                    a, b2 = half * 64, (half + 1) * 64
                    p.op("v", lambda e, hh=hh, a=a, b2=b2: e.tensor_copy(
                        wqd_hi[hh][:, :, a:b2],
                        wq_f[:, :, hh * 64:(hh + 1) * 64]),
                        waits=(wsp if first else []))
                    first = False
                    p.op("v", lambda e, hh=hh, a=a, b2=b2: e.scalar_tensor_tensor(
                        wqd_lo[hh][:, :, a:b2], wqd_hi[hh][:, :, a:b2], -1.0,
                        wq_f[:, :, hh * 64:(hh + 1) * 64], ALU.mult, ALU.add))
                    p.op("v", lambda e, hh=hh, a=a, b2=b2: e.tensor_copy(
                        wkd_hi[hh][:, :, a:b2],
                        wk_f[:, :, hh * 64:(hh + 1) * 64]))
                    p.op("v", lambda e, hh=hh, a=a, b2=b2: e.scalar_tensor_tensor(
                        wkd_lo[hh][:, :, a:b2], wkd_hi[hh][:, :, a:b2], -1.0,
                        wk_f[:, :, hh * 64:(hh + 1) * 64], ALU.mult, ALU.add))
            p.op("v", lambda e: e.tensor_copy(
                wv_hi[:, :, :], wv_f[:, :, :]))
            sd = p.op("v", lambda e: e.scalar_tensor_tensor(
                wv_lo[:, :, :], wv_hi[:, :, :], -1.0, wv_f[:, :, :],
                ALU.mult, ALU.add))
            m_bv = p.op("t", lambda e: e.matmul(
                pb[7][:, 0:128], ones_row[:, :], bv_row[0:1, :],
                start=True, stop=True),
                waits=[(s_ve, sd)] + bank_free[7], inc=(s_pe, 1))
            split_done[pr] = sd
            if pr + 1 < 8:
                load_pair_slices(pr + 1)
            vbv = p.op("v", lambda e: e.tensor_copy(bv_bc[:, :], pb[7][:, 0:128]),
                       waits=[(s_pe, m_bv)])
            bank_free[7] = [(s_ve, vbv)]

            for hh in range(2):
                kterms = [(wkd_hi[hh], latT_hi), (wkd_hi[hh], latT_lo),
                          (wkd_lo[hh], latT_hi)]
                for kt in range(8):
                    bank = 4 + (kt % 2)
                    n_mm = 0
                    for wsl_, lat in kterms:
                        for lc2 in range(2):
                            w = ([(s_ve, sd), (s_lat, lat_full)] +
                                 bank_free[bank]) if n_mm == 0 else []
                            p.op("t", lambda e, wsl_=wsl_, lat=lat, lc2=lc2,
                                 kt=kt, bank=bank, st=(n_mm == 0),
                                 sp=(n_mm == 5): e.matmul(
                                     pb[bank][:, :], wsl_[:, lc2, :],
                                     lat[:, lc2, kt * 512:(kt + 1) * 512],
                                     start=st, stop=sp),
                                 waits=w, inc=((s_pe, 1) if n_mm == 5 else None))
                            n_mm += 1
                    mmk = s_pe.n
                    s1 = p.op("s", lambda e, bank=bank, pr=pr, hh=hh:
                              e.activation(et0[:, :], pb[bank][:, :],
                                           AF.Identity,
                                           bias=bkd[:, 2 * pr + hh:2 * pr + hh + 1]),
                              waits=[(s_pe, mmk), (s_ve, s_ve.n)], inc=(s_se, 1))
                    bank_free[bank] = [(s_se, s1)]
                    p.op("v", lambda e, hh=hh, kt=kt: e.tensor_copy(
                        KTs[hh][:, kt * 512:(kt + 1) * 512], et0[:, :]),
                        waits=[(s_se, s1)])
                    p.op("v", lambda e, hh=hh, kt=kt: e.scalar_tensor_tensor(
                        KTs[hh][64:128, kt * 512:(kt + 1) * 512],
                        KTs[hh][64:128, kt * 512:(kt + 1) * 512], -1.0,
                        et0[64:128, :], ALU.mult, ALU.add))
            vterms = [(wv_hi, latT_hi), (wv_hi, latT_lo), (wv_lo, latT_hi)]
            for kc4 in range(8):
                bank = 4 + (kc4 % 2)
                for qtr in range(4):
                    kc = kc4 * 4 + qtr
                    n_mm = 0
                    for wsl_, lat in vterms:
                        for lc2 in range(2):
                            w = (bank_free[bank] if (qtr == 0 and n_mm == 0)
                                 else [])
                            p.op("t", lambda e, wsl_=wsl_, lat=lat, lc2=lc2,
                                 kc=kc, bank=bank, qtr=qtr, st=(n_mm == 0),
                                 sp=(n_mm == 5): e.matmul(
                                     pb[bank][:, qtr * 128:(qtr + 1) * 128],
                                     lat[:, lc2, kc * 128:(kc + 1) * 128],
                                     wsl_[:, lc2, :], start=st, stop=sp),
                                 waits=w, inc=((s_pe, 1) if n_mm == 5 else None))
                            n_mm += 1
                mmv = s_pe.n
                for qtr in range(4):
                    kc = kc4 * 4 + qtr
                    vv = p.op("v", lambda e, bank=bank, qtr=qtr: e.tensor_tensor(
                        et0[:, 0:128], pb[bank][:, qtr * 128:(qtr + 1) * 128],
                        bv_bc[:, :], ALU.add),
                        waits=[(s_pe, mmv), (s_ve, vbv)] if qtr == 0 else
                        [(s_pe, mmv)])
                    for hh in range(2):
                        p.op("v", lambda e, hh=hh, kc=kc: e.tensor_copy(
                            V65[hh][:, kc, 0:64],
                            et0[:, hh * 64:(hh + 1) * 64]))
                        p.op("v", lambda e, hh=hh, kc=kc: e.scalar_tensor_tensor(
                            Vlo[hh][:, kc, 0:64], V65[hh][:, kc, 0:64], -1.0,
                            et0[:, hh * 64:(hh + 1) * 64], ALU.mult, ALU.add))
                bank_free[bank] = [(s_ve, s_ve.n)]

            for hh in range(2):
                qterms = [(wqd_hi[hh], xn2T_hi), (wqd_hi[hh], xn2T_lo),
                          (wqd_lo[hh], xn2T_hi)]
                for t4 in range(4):
                    bank = 4 + (t4 % 2)
                    n_mm = 0
                    for wsl_, xn in qterms:
                        for ic in range(8):
                            w = ([(s_ve, sd)] + bank_free[bank]) \
                                if n_mm == 0 else []
                            p.op("t", lambda e, wsl_=wsl_, xn=xn, ic=ic, t4=t4,
                                 bank=bank, st=(n_mm == 0), sp=(n_mm == 23):
                                 e.matmul(
                                     pb[bank][:, :], wsl_[:, ic, :],
                                     xn[:, ic, t4 * 512:(t4 + 1) * 512],
                                     start=st, stop=sp),
                                 waits=w, inc=((s_pe, 1) if n_mm == 23 else None))
                            n_mm += 1
                    mmq = s_pe.n
                    s1 = p.op("s", lambda e, bank=bank, pr=pr, hh=hh:
                              e.activation(et0[:, :], pb[bank][:, :],
                                           AF.Identity,
                                           bias=bqd[:, 2 * pr + hh:2 * pr + hh + 1]),
                              waits=[(s_pe, mmq), (s_ve, s_ve.n)], inc=(s_se, 1))
                    bank_free[bank] = [(s_se, s1)]
                    p.op("v", lambda e, hh=hh, t4=t4: e.tensor_copy(
                        QhiD[hh][:, t4 * 512:(t4 + 1) * 512], et0[:, :]),
                        waits=[(s_se, s1)])
                    p.op("v", lambda e, hh=hh, t4=t4: e.scalar_tensor_tensor(
                        QloD[hh][:, t4 * 512:(t4 + 1) * 512],
                        QhiD[hh][:, t4 * 512:(t4 + 1) * 512], -1.0, et0[:, :],
                        ALU.mult, ALU.add))
            sq_bufs = (et0, sqs)
            sq_free = {0: [], 1: []}
            tmp_col = (6, 7)
            seg_i = 0
            first_norm = [(s_pe, s_pe.n), (s_ve, s_ve.n)]
            for hh in range(2):
                for src_t, seg, col, width in ((KTs[hh], 8, hh, 512),
                                               (QhiD[hh], 4, 3 + hh, 512)):
                    for kt in range(seg):
                        par = seg_i % 2
                        bnk = 6 + par
                        s2 = p.op("s", lambda e, src_t=src_t, kt=kt, width=width,
                                  par=par: e.activation(
                                      sq_bufs[par][:, 0:width],
                                      src_t[:, kt * width:(kt + 1) * width],
                                      AF.Square),
                                  waits=first_norm + sq_free[par],
                                  inc=(s_se, 1))
                        first_norm = []
                        m_n = p.op("t", lambda e, width=width, par=par, bnk=bnk:
                                   e.matmul(
                                       pb[bnk][0:1, 0:width], ones_col[:, :],
                                       sq_bufs[par][:, 0:width],
                                       start=True, stop=True),
                                   waits=[(s_se, s2)] + bank_free[bnk],
                                   inc=(s_pe, 1))
                        sq_free[par] = [(s_pe, m_n)]
                        p.op("v", lambda e, width=width, par=par, bnk=bnk:
                             e.reduce_max(
                                 mqk[0:1, tmp_col[par]:tmp_col[par] + 1],
                                 pb[bnk][0:1, 0:width], axis=AXL.X),
                             waits=[(s_pe, m_n)])
                        vkn = p.op("v", lambda e, col=col, kt=kt, par=par: (
                            e.tensor_copy(mqk[0:1, col:col + 1],
                                          mqk[0:1, tmp_col[par]:tmp_col[par] + 1])
                            if kt == 0 else
                            e.tensor_tensor(mqk[0:1, col:col + 1],
                                            mqk[0:1, col:col + 1],
                                            mqk[0:1, tmp_col[par]:tmp_col[par] + 1],
                                            ALU.max)))
                        bank_free[bnk] = [(s_ve, vkn)]
                        seg_i += 1

                p.op("v", lambda e, hh=hh: e.tensor_tensor(
                    mqk[0:1, 5:6], mqk[0:1, hh:hh + 1], mqk[0:1, 3 + hh:4 + hh],
                    ALU.mult))
                p.op("s", lambda e: e.activation(
                    mqk[0:1, 2:3], mqk[0:1, 5:6], AF.Ln),
                    waits=[(s_ve, s_ve.n)])
                p.op("s", lambda e: e.activation(
                    mqk[0:1, 2:3], mqk[0:1, 2:3], AF.Exp, scale=0.5))
                vb = p.op("v", lambda e: e.tensor_single_scalar(
                    mqk[0:1, 5:6], mqk[0:1, 2:3],
                    -0.0625 * 1.002, ALU.mult),
                    waits=[(s_se, s_se.n)])
                m_b = p.op("t", lambda e: e.matmul(
                    pb[6][:, 0:1], ones_row[:, :], mqk[0:1, 5:6],
                    start=True, stop=True),
                    waits=[(s_ve, vb)] + bank_free[6], inc=(s_pe, 1))
                vnb = p.op("v", lambda e, hh=hh: e.tensor_copy(
                    negb[hh][:, :], pb[6][:, 0:1]), waits=[(s_pe, m_b)])
                bank_free[6] = [(s_ve, vnb)]

            for hh in range(2):
                hg = pr * 2 + hh
                for qg in range(2):
                    qts = (2 * qg, 2 * qg + 1)
                    iq = (hh * 2 + qg) % 2
                    obs = (2, 3) if iq == 0 else (4, 5)
                    exp_done = {}
                    split_v = {}
                    split_g = {}
                    av_done = {}

                    def emit_scg(kc, hh=hh, qts=qts):
                        for qi, qt in enumerate(qts):
                            sbk = ((0, 1) if kc % 2 == 0 else (6, 7))[qi]
                            w = list(bank_free[sbk]) if True else []
                            p.op("t", lambda e, hh=hh, kc=kc, qt=qt, sbk=sbk:
                                 e.matmul(pb[sbk][:, :],
                                          KTs[hh][:, kc * 128:(kc + 1) * 128],
                                          QhiD[hh][:, qt * 512:(qt + 1) * 512],
                                          start=True, stop=False),
                                 waits=w)
                            msc = p.op("t", lambda e, hh=hh, kc=kc, qt=qt,
                                       sbk=sbk: e.matmul(
                                           pb[sbk][:, :],
                                           KTs[hh][:, kc * 128:(kc + 1) * 128],
                                           QloD[hh][:, qt * 512:(qt + 1) * 512],
                                           start=False, stop=True),
                                       inc=(s_pe, 1))
                            slot = (kc % 2) * 2 + qi
                            se_ = p.op("s", lambda e, slot=slot, sbk=sbk, hh=hh:
                                       e.activation(
                                           pT_f[slot][:, :], pb[sbk][:, :],
                                           AF.Exp, bias=negb[hh][:, :],
                                           scale=0.125),
                                       waits=[(s_pe, msc)] +
                                       split_v.get((kc - 2, qi), []) +
                                       split_g.get((kc - 2, qi), []),
                                       inc=(s_se, 1))
                            bank_free[sbk] = [(s_se, se_)]
                            exp_done[(kc, qi)] = se_

                    def emit_splitg(kc):
                        for qi in range(2):
                            fsl = (kc % 2) * 2 + qi
                            slot = (kc % 3) * 2 + qi
                            if qi == 0:
                                vh = p.op("v", lambda e, slot=slot, fsl=fsl:
                                          e.tensor_copy(
                                              pT_hi[slot][:, :], pT_f[fsl][:, :]),
                                          waits=[(s_se, exp_done[(kc, qi)])] +
                                          av_done.get(kc - 3, []))
                                gl = p.op("v", lambda e, slot=slot, fsl=fsl:
                                          e.scalar_tensor_tensor(
                                              pT_lo[slot][:, :],
                                              pT_hi[slot][:, :], -1.0,
                                              pT_f[fsl][:, :],
                                              ALU.mult, ALU.add))
                                split_v[(kc, qi)] = [(s_ve, vh)]
                                split_g[(kc, qi)] = [(s_ve, gl)]
                            else:
                                vh = p.op("s", lambda e, slot=slot, fsl=fsl:
                                          e.activation(
                                              pT_hi[slot][:, :], pT_f[fsl][:, :],
                                              AF.Copy, bias=0.0, scale=1.0),
                                          waits=av_done.get(kc - 3, []))
                                gl = p.op("g", lambda e, slot=slot, fsl=fsl:
                                          e.tensor_tensor(
                                              pT_lo[slot][:, :], pT_f[fsl][:, :],
                                              pT_hi[slot][:, :], ALU.subtract),
                                          waits=[(s_se, vh)] +
                                          av_done.get(kc - 3, []))
                                split_v[(kc, qi)] = [(s_se, vh)]
                                split_g[(kc, qi)] = [(s_gc, gl)]

                    def emit_avg(kc, hh=hh, qts=qts, obs=obs):
                        avw = (split_v[(kc, 0)] + split_g[(kc, 0)] +
                               split_v[(kc, 1)] + split_g[(kc, 1)])
                        if kc == 0:
                            avw = avw + bank_free[obs[0]] + bank_free[obs[1]]
                        slots = ((kc % 3) * 2, (kc % 3) * 2 + 1)
                        mav = None
                        if kc < 31:
                            for qi in range(2):
                                p.op("t", lambda e, hh=hh, kc=kc, s_=slots[qi],
                                     ob=obs[qi], st=(kc == 0): e.matmul(
                                         pb[ob][0:65, :], V65[hh][:, kc, 0:65],
                                         pT_hi[s_][:, :], start=st, stop=False),
                                     waits=(avw if qi == 0 else []))
                            for qi in range(2):
                                p.op("t", lambda e, hh=hh, kc=kc, s_=slots[qi],
                                     ob=obs[qi]: e.matmul(
                                         pb[ob][0:65, :], V65[hh][:, kc, 0:65],
                                         pT_lo[s_][:, :], start=False,
                                         stop=False))
                            for qi in range(2):
                                mav = p.op("t", lambda e, hh=hh, kc=kc,
                                           s_=slots[qi], ob=obs[qi]: e.matmul(
                                               pb[ob][0:64, :],
                                               Vlo[hh][:, kc, 0:64],
                                               pT_hi[s_][:, :], start=False,
                                               stop=False),
                                           inc=((s_pe, 1) if qi == 1 else None))
                        else:
                            for qi in range(2):
                                p.op("t", lambda e, hh=hh, kc=kc, s_=slots[qi],
                                     ob=obs[qi]: e.matmul(
                                         pb[ob][0:64, :], Vlo[hh][:, kc, 0:64],
                                         pT_hi[s_][:, :], start=False,
                                         stop=False),
                                     waits=(avw if qi == 0 else []))
                            for qi in range(2):
                                p.op("t", lambda e, hh=hh, kc=kc, s_=slots[qi],
                                     ob=obs[qi]: e.matmul(
                                         pb[ob][0:65, :], V65[hh][:, kc, 0:65],
                                         pT_hi[s_][:, :], start=False,
                                         stop=False))
                            for qi in range(2):
                                mav = p.op("t", lambda e, hh=hh, kc=kc,
                                           s_=slots[qi], ob=obs[qi]: e.matmul(
                                               pb[ob][0:65, :],
                                               V65[hh][:, kc, 0:65],
                                               pT_lo[s_][:, :], start=False,
                                               stop=True),
                                           inc=((s_pe, 1) if qi == 1 else None))
                        av_done[kc] = [(s_pe, mav)]

                    emit_scg(0)
                    emit_scg(1)
                    emit_splitg(0)
                    emit_splitg(1)
                    for kc in range(32):
                        if kc + 2 < 32:
                            emit_scg(kc + 2)
                        if kc == 4:
                            while pend_den:
                                flush_denominator()
                        emit_avg(kc)
                        if kc + 2 < 32:
                            emit_splitg(kc + 2)
                    mo = s_pe.n
                    pend_den.append((obs[0], hg, 2 * qg, mo))
                    pend_den.append((obs[1], hg, 2 * qg + 1, mo))
            while pend_den:
                flush_denominator()
        return (s_ot[0].n, s_ot[1].n)

    oT_free = {}

    def emit_wo_resid(ot_total, x_res_dram, x_out_dram, y_out, do_ln3, xqT_out):
        ot_waits = [(s_ot[0], ot_total[0]), (s_ot[1], ot_total[1])]
        """out = x_res + O @ Wo + bo, chunk by chunk; optionally LN3+quant
        into xqT_out. Writes x_out_dram (and y_out if not None)."""
        PB = 60 * KB
        _c = [PB]

        def sbw(name, shape, dtype):
            off = (_c[0] + 31) // 32 * 32
            _c[0] = off + int(np.prod(shape[1:])) * mybir.dt.size(dtype)
            assert _c[0] <= 160 * KB, (name, _c[0])
            return sb(name, shape, dtype, off)

        wo_f = sbw("wo_f", [128, 8, D], F32)
        wo_hi = sbw("wo_hi", [128, 8, D], BF16)
        wo_lo = sbw("wo_lo", [128, 8, D], BF16)
        qt3 = ([sb("qt3_f0", [128, D], F32, PB),
                sb("qt3_f1", [128, D], F32, PB + 4 * KB)],
               sb("qt3_i", [128, D], I32, PB + 8 * KB),
               sb("qt3_e", [128, D], F32, PB + 12 * KB),
               sb("qt3_c", [128, D], F32, PB + 16 * KB),
               sb("xq3_tm", [128, D], BF16, PB + 20 * KB),
               [sb("qt3_b0", [128, D], F32, PB + 22 * KB),
                sb("qt3_b1", [128, D], F32, PB + 26 * KB)])
        oT_b = [sbw(f"oT_b{i}", [128, 8, 128], BF16) for i in range(4)]
        bo_row = sbw("bo_row", [1, D], F32)
        bo_bc = sbw("bo_bc", [128, D], F32)
        x3 = [sbw(f"x3_{i}", [128, D], F32) for i in range(2)]
        dn3 = sbw("dn3", [128, D], F32)

        lwo = p.op("y", lambda e: e.dma_start(
            out=wo_f[:, :, :], in_=dwo.rearrange("(a p) o -> p a o", p=128)),
            waits=ot_waits, inc=(s_wo, 16))
        lbo = p.op("y", lambda e: e.dma_start(out=bo_row[0:1, :], in_=dbo[None, :]),
                   inc=(s_wo, 16))
        wo_done = s_wo.n
        p.op("v", lambda e: e.tensor_copy(wo_hi[:, :, :], wo_f[:, :, :]),
             waits=[(s_wo, wo_done)])
        vwo = p.op("v", lambda e: e.scalar_tensor_tensor(
            wo_lo[:, :, :], wo_hi[:, :, :], -1.0, wo_f[:, :, :],
            ALU.mult, ALU.add))
        for half in range(2):
            m_b = p.op("t", lambda e, half=half: e.matmul(
                pb[6][:, :], ones_row[:, :],
                bo_row[0:1, half * 512:(half + 1) * 512],
                start=True, stop=True),
                waits=[(s_wo, wo_done)] + bank_free[6], inc=(s_pe, 1))
            vb = p.op("v", lambda e, half=half: e.tensor_copy(
                bo_bc[:, half * 512:(half + 1) * 512], pb[6][:, :]),
                waits=[(s_pe, m_b)])
            bank_free[6] = [(s_ve, vb)]

        xin = x_res_dram.rearrange("(c p) d -> c p d", p=128)
        xout = x_out_dram.rearrange("(c p) d -> c p d", p=128)
        yout = (y_out.rearrange("(c p) d -> c p d", p=128)
                if y_out is not None else None)
        oT_hi_src = doT_hi.rearrange("(a p) t -> p a t", p=128)
        oT_lo_src = doT_lo.rearrange("(a p) t -> p a t", p=128)
        o_ld = {}
        x_ld = {}
        xq_ev3 = []
        pend_ln3 = None
        qtb_free3 = {0: 0, 1: 0}
        for tc in range(NT):
            par = tc % 2
            w = o_ld.get(tc - 2, [])
            l1 = p.op("y", lambda e, tc=tc, par=par: e.dma_start(
                out=oT_b[par][:, :, :],
                in_=oT_hi_src[:, :, tc * 128:(tc + 1) * 128]),
                waits=w + (ot_waits if tc < 2 else []),
                inc=(s_o3[par], 16))
            l2 = p.op("y", lambda e, tc=tc, par=par: e.dma_start(
                out=oT_b[2 + par][:, :, :],
                in_=oT_lo_src[:, :, tc * 128:(tc + 1) * 128]),
                inc=(s_o3[par], 16))
            lx = p.op("y", lambda e, tc=tc, par=par: e.dma_start(
                out=x3[par][:, :], in_=xin[tc]),
                waits=x_ld.get(tc - 2, []), inc=(s_lx[par], 16))
            oterms = [(0, wo_hi), (0, wo_lo), (2, wo_hi)]
            mm_end = {}
            for half in range(2):
                bank = 4 + (half % 2)
                n_mm = 0
                for obi, wo_ in oterms:
                    for dc in range(8):
                        w2 = []
                        if n_mm == 0:
                            w2 = ([(s_o3[par], l2), (s_ve, vwo)] +
                                  bank_free[bank])
                        p.op("t", lambda e, obi=obi, wo_=wo_, dc=dc, par=par,
                             half=half, bank=bank, st=(n_mm == 0),
                             sp=(n_mm == 23): e.matmul(
                                 pb[bank][:, :], oT_b[obi + par][:, dc, :],
                                 wo_[:, dc, half * 512:(half + 1) * 512],
                                 start=st, stop=sp),
                             waits=w2, inc=((s_pe, 1) if n_mm == 23 else None))
                        n_mm += 1
                mm_end[half] = s_pe.n
                vv = p.op("v", lambda e, bank=bank, half=half: e.tensor_tensor(
                    dn3[:, half * 512:(half + 1) * 512], pb[bank][:, :],
                    bo_bc[:, half * 512:(half + 1) * 512], ALU.add),
                    waits=[(s_pe, mm_end[half])])
                vv = p.op("v", lambda e, par=par, half=half: e.tensor_tensor(
                    x3[par][:, half * 512:(half + 1) * 512],
                    x3[par][:, half * 512:(half + 1) * 512],
                    dn3[:, half * 512:(half + 1) * 512], ALU.add),
                    waits=[(s_lx[par], lx)])
                bank_free[bank] = [(s_ve, vv)]
            o_ld[tc] = [(s_ve, s_ve.n)]
            st1 = p.op("g", lambda e, tc=tc, par=par: e.dma_start(
                out=xout[tc], in_=x3[par][:, :]),
                waits=[(s_ve, s_ve.n)], inc=(s_sto[par], 16))
            if yout is not None:
                st1 = p.op("g", lambda e, tc=tc, par=par: e.dma_start(
                    out=yout[tc], in_=x3[par][:, :]), inc=(s_sto[par], 16))
            x_ld[tc] = [(s_sto[par], st1)]
            if do_ln3:
                vm, sc2 = ln_quant_p1(x3[par], invact1[:, tc:tc + 1], tc,
                                      ow_waits=[(s_sto[par], st1)],
                                      scratch=qt3, qtb_free=qtb_free3)
                if pend_ln3 is not None:
                    _, ev = ln_quant_p2(xqT_out, pend_ln3[0], pend_ln3[0],
                                        pend_ln3[1], scratch=qt3,
                                        qtb_free=qtb_free3)
                    xq_ev3.append(ev)
                pend_ln3 = (tc, sc2)
                x_ld[tc] = [(s_ve, vm)]
        if pend_ln3 is not None:
            _, ev = ln_quant_p2(xqT_out, pend_ln3[0], pend_ln3[0], pend_ln3[1],
                                scratch=qt3, qtb_free=qtb_free3)
            xq_ev3.append(ev)
        return (s_sto[0].n, s_sto[1].n, s_pe.n, xq_ev3)

    p.op("v", lambda e: e.tensor_single_scalar(scg_all[:, :], invact1[:, :],
                                               iws_g[:, :], ALU.mult),
         waits=[(s_ve, max(ws_ready, xq_done))])
    vs = p.op("v", lambda e: e.tensor_single_scalar(scu_all[:, :], invact1[:, :],
                                                    iws_u[:, :], ALU.mult),
              inc=(s_ve, 1))

    if STAGE == 1:
        f0, f1 = emit_ffn(xq1T, dx, dx_cur, dy, scg_all, scu_all, scd_all,
                          [(s_ve, xq_done)], xq_ev1,
                          dq2T[0], sc_gate=[(s_ve, vs)], wst=prep_st,
                          gu_gate=[(s_se, prep_s_last), prep_v_last] +
                          prep_done_gd,
                          ev_gate=list(prep_done_gd))
        p.op("g", lambda e: e.nop(), waits=[(s_sto[0], f0), (s_sto[1], f1)])
        p.op("y", lambda e: e.nop(), waits=[(s_sto[0], f0), (s_sto[1], f1)])
    elif STAGE == 2:
        f0, f1 = emit_ffn(xq1T, dx, dx_cur, None, scg_all, scu_all, scd_all,
                          [(s_ve, xq_done)], xq_ev1,
                          dq2T[0], sc_gate=[(s_ve, vs)], wst=prep_st,
                          gu_gate=[(s_se, prep_s_last), prep_v_last] +
                          prep_done_gd,
                          ev_gate=list(prep_done_gd))
        ot_total = emit_attention(dx_cur, [(s_sto[0], f0), (s_sto[1], f1)])
        g0, g1, _, _ = emit_wo_resid(ot_total, dx_cur, dx_cur2, dy, False, None)
        p.op("g", lambda e: e.nop(), waits=[(s_sto[0], g0), (s_sto[1], g1)])
        p.op("y", lambda e: e.nop(), waits=[(s_sto[0], g0), (s_sto[1], g1)])
    elif STAGE == 3:
        f0, f1 = emit_ffn(xq1T, dx, dx_cur, None, scg_all, scu_all, scd_all,
                          [(s_ve, xq_done)], xq_ev1,
                          dq2T[0], sc_gate=[(s_ve, vs)], wst=prep_st,
                          gu_gate=[(s_se, prep_s_last), prep_v_last] +
                          prep_done_gd,
                          ev_gate=list(prep_done_gd))
        ot_total = emit_attention(dx_cur, [(s_sto[0], f0), (s_sto[1], f1)])
        g0, g1, p3_pe, xq_ev3 = emit_wo_resid(ot_total, dx_cur, dx_cur2, None,
                                              True, xq1T)
        p.op("v", lambda e: e.tensor_single_scalar(
            scg_all[:, :], invact1[:, :], iws_g[:, :], ALU.mult))
        vs2 = p.op("v", lambda e: e.tensor_single_scalar(
            scu_all[:, :], invact1[:, :], iws_u[:, :], ALU.mult))
        h0, h1 = emit_ffn(xq1T, dx_cur2, dy, None, scg_all, scu_all, scd_all,
                          [(s_ot[0], ot_total[0]), (s_ot[1], ot_total[1])],
                          xq_ev3, dq2T[1], sc_gate=[(s_ve, vs2)],
                          gu_gate=[(s_pe, p3_pe), (s_ve, xq_ev3[NT - 1])],
                          ev_gate=[(s_pe, p3_pe), (s_ve, xq_ev3[NT - 1])])
        p.op("g", lambda e: e.nop(), waits=[(s_sto[0], h0), (s_sto[1], h1)])
        p.op("y", lambda e: e.nop(), waits=[(s_sto[0], h0), (s_sto[1], h1)])

    p.emit()
    return nc


def kernel(L0_attn_x, L0_attn_Wq, L0_attn_bq, L0_attn_Wl, L0_attn_bl,
           L0_attn_Wk, L0_attn_bk, L0_attn_Wv, L0_attn_bv, L0_attn_Wo,
           L0_attn_bo, L0_ffn_w_gate, L0_ffn_w_up, L0_ffn_w_down):
    x = np.ascontiguousarray(np.asarray(L0_attn_x, dtype=np.float32))
    assert x.shape == (4, 4096, 1024)
    idn = np.eye(128)
    in_maps = []
    for c in range(N_CORES):
        b, h = c // 2, c % 2
        m = {
            "x": np.ascontiguousarray(x[b, h * T:(h + 1) * T, :]),
            "w_gate": np.ascontiguousarray(np.asarray(L0_ffn_w_gate, np.float32)),
            "w_up": np.ascontiguousarray(np.asarray(L0_ffn_w_up, np.float32)),
            "w_down": np.ascontiguousarray(np.asarray(L0_ffn_w_down, np.float32)),
            "idn_bf16": idn.astype(ml_dtypes.bfloat16),
        }
        if STAGE >= 2:
            m.update({
                "Wq": np.ascontiguousarray(np.asarray(L0_attn_Wq, np.float32)),
                "Wl": np.ascontiguousarray(np.asarray(L0_attn_Wl, np.float32)),
                "Wk": np.ascontiguousarray(np.asarray(L0_attn_Wk, np.float32)),
                "Wv": np.ascontiguousarray(np.asarray(L0_attn_Wv, np.float32)),
                "Wo": np.ascontiguousarray(np.asarray(L0_attn_Wo, np.float32)),
                "bq": np.ascontiguousarray(np.asarray(L0_attn_bq, np.float32)),
                "bl": np.ascontiguousarray(np.asarray(L0_attn_bl, np.float32)),
                "bk": np.ascontiguousarray(np.asarray(L0_attn_bk, np.float32)),
                "bv": np.ascontiguousarray(np.asarray(L0_attn_bv, np.float32)),
                "bo": np.ascontiguousarray(np.asarray(L0_attn_bo, np.float32)),
                "idn_f32": idn.astype(np.float32),
            })
        in_maps.append(m)
    nc = build_nc()
    res = run_bass_kernel_spmd(nc, in_maps, core_ids=list(range(N_CORES)))
    out = np.empty((4, 4096, 1024), dtype=np.float32)
    for c in range(N_CORES):
        b, h = c // 2, c % 2
        out[b, h * T:(h + 1) * T, :] = res.results[c]["y"]
    return out
